# version 15
# baseline (speedup 1.0000x reference)
import numpy as np
import ml_dtypes
from contextlib import ExitStack

import concourse.bass as bass
import concourse.mybir as mybir
from concourse.bass_utils import run_bass_kernel_spmd

F32 = mybir.dt.float32
BF16 = mybir.dt.bfloat16
AF = mybir.ActivationFunctionType
ALU = mybir.AluOpType
AX = mybir.AxisListType.X

B, S, D = 4, 2048, 2048
H, HD = 16, 128
NSLOT = 16
NT = 8
FE = 1408
FS = 2816
NE = 8
INV_SQRT_HD = 1.0 / float(np.sqrt(HD))
EPS = 1e-5
CONST_VALS = [EPS, -0.5, INV_SQRT_HD, -INV_SQRT_HD, 1.0 / D, -1.0 / D,
              -1.0e30, 1.0, 0.0, 2.0]


class Plan:
    ENGINES = ("pe", "dv", "ac", "gp", "sy")

    def __init__(self, sems, cnt):
        self.q = {e: [] for e in self.ENGINES}
        self.sems = sems
        self.cnt = cnt

    def op(self, eng, fn, inc=None, incn=1):
        ev = None
        if inc is not None:
            self.cnt[inc] += incn
            ev = (inc, self.cnt[inc])
        self.q[eng].append(("op", fn, inc, incn))
        return ev

    def wait(self, eng, ev):
        if ev is None:
            return
        self.q[eng].append(("wait", ev[0], ev[1]))

    def opw(self, eng, fn):
        ev = self.op(eng, fn, inc=eng)
        self.wait(eng, ev)
        return ev

    def dma(self, eng, out_ap, in_ap, inc, accum=False):
        def fn(e, o=out_ap, i=in_ap, a=accum):
            if a:
                return e.dma_start(o, i, accum_op=ALU.add)
            return e.dma_start(o, i)
        return self.op(eng, fn, inc=inc, incn=16)

    def replay(self, eng_name, e):
        for item in self.q[eng_name]:
            if item[0] == "op":
                _, fn, inc, incn = item
                inst = fn(e)
                if inc is not None:
                    inst.then_inc(self.sems[inc], incn)
            else:
                _, sem, val = item
                e.wait_ge(self.sems[sem], val)

    def emit(self, nc):
        with nc.Block() as blk:
            if self.q["gp"]:
                blk.gpsimd(lambda e: self.replay("gp", e))
            if self.q["sy"]:
                blk.sync(lambda e: self.replay("sy", e))
            if self.q["pe"]:
                blk.tensor(lambda e: self.replay("pe", e))
            if self.q["dv"]:
                blk.vector(lambda e: self.replay("dv", e))
            if self.q["ac"]:
                blk.scalar(lambda e: self.replay("ac", e))


def build(stage=4, taps=()):
    nc = bass.Bass("TRN2", target_bir_lowering=False)

    def din(name, shape, dt=F32):
        return nc.dram_tensor(name, shape, dt, kind="ExternalInput").ap()

    x_in = din("x", [S, D])
    wq = din("wq", [D, D])
    wk = din("wk", [D, D])
    wv = din("wv", [D, D])
    wo = din("wo", [D, D])
    gwt = din("gwt", [D, NE])
    gate_t = din("gate_t", [NE, D, FE])
    up_t = din("up_t", [NE, D, FE])
    down_t = din("down_t", [NE, FE, D])
    shg_t = din("shg_t", [D, FS])
    shu_t = din("shu_t", [D, FS])
    shd_t = din("shd_t", [FS, D])
    cosf_in = din("cosf", [128, S])
    sinf_in = din("sinf", [128, S])
    idt_in = din("idt", [128, 128], BF16)
    idtf_in = din("idtf", [128, 128])
    psw_in = din("psw", [128, 128], BF16)
    tri_in = din("tri", [128, 128])
    tri01_in = din("tri01", [128, 128], BF16)
    onesb_in = din("onesb", [128, 1], BF16)
    flg_in = din("flg", [128, 1])
    onesf_in = din("onesf", [128, 1])
    onesr_in = din("onesr", [1, 128])
    cvec_in = din("cvec", [128, len(CONST_VALS)])

    out = nc.dram_tensor("out", [1024, D], F32, kind="ExternalOutput").ap()
    tk = lambda n: "ExternalOutput" if n in taps else "Internal"
    ctx_dram = nc.dram_tensor("d_ctx", [H, NT, 128, 128], BF16, kind=tk("d_ctx")).ap()
    tap_t = {}
    for nm, shp, dt in (("d_h1", [128, NSLOT * 2048], BF16),
                        ("d_kfm", [128, 4 * 2048], BF16),
                        ("d_qfm", [128, 4 * 1024], BF16),
                        ("d_vsb", [128, NSLOT * 512], BF16),
                        ("d_h2", [128, NSLOT * 1024], BF16),
                        ("d_wmat", [128, NT * NE], F32),
                        ("d_stat", [128, 96], F32),
                        ("d_acta", [128, NT * FE], BF16),
                        ("d_logit", [128, NT * NE], F32)):
        if nm in taps:
            tap_t[nm] = nc.dram_tensor(nm, shp, dt, kind="ExternalOutput").ap()

    esP = ExitStack()

    def sbP(name, shape, dt):
        return esP.enter_context(nc.sbuf_tensor(name, shape, dt))

    def psP(name, shape, dt=F32):
        return esP.enter_context(nc.psum_tensor(name, shape, dt))

    IDT = sbP("IDT", [128, 128], BF16)
    IDTF = sbP("IDTF", [128, 128], F32)
    PSW = sbP("PSW", [128, 128], BF16)
    TRI = sbP("TRI", [128, 128], F32)
    TRI01 = sbP("TRI01", [128, 128], BF16)
    ONESB = sbP("ONESB", [128, 1], BF16)
    FLG = sbP("FLG", [128, 1], F32)
    ONESF = sbP("ONESF", [128, 1], F32)
    ONESR = sbP("ONESR", [1, 128], F32)
    GWTB = sbP("GWTB", [128, 16, NE], F32)
    KV2 = sbP("KV2", [128, 16384], BF16)
    GRID = sbP("GRID", [128, 256], F32)
    STAT = sbP("STAT", [128, 96], F32)
    WMAT = sbP("WMAT", [128, NT * NE], F32)
    SBC = sbP("SBC", [128, NE], F32)
    SRV = sbP("SRV", [1, NE], F32)
    CVEC = sbP("CVEC", [128, len(CONST_VALS)], F32)
    for ci, cv in enumerate(CONST_VALS):
        nc.const_aps.register(F32, cv, CVEC[:, ci:ci + 1])

    PSBIG = psP("PSBIG", [128, 2048])
    PS4 = psP("PS4", [128, 512])
    PS5 = psP("PS5", [128, 512])
    PSB6 = psP("PSB6", [128, 1024], BF16)
    PSB7 = psP("PSB7", [128, 1024], BF16)

    sems = {}
    cnt = {}
    for s_ in ("qW", "qX", "qS", "qA", "pe", "dv", "ac"):
        sems[s_] = esP.enter_context(nc.semaphore(s_))
        cnt[s_] = 0

    x_r = x_in.rearrange("(t p) d -> p t d", p=128)
    wq_r = wq.rearrange("(c p) n -> p c n", p=128)
    wk_r = wk.rearrange("(c p) n -> p c n", p=128)
    wv_r = wv.rearrange("(c p) n -> p c n", p=128)
    wo_r = wo.rearrange("(c p) n -> p c n", p=128)
    gwt_r = gwt.rearrange("(c p) n -> p c n", p=128)
    shg_r = shg_t.rearrange("(c p) n -> p c n", p=128)
    shu_r = shu_t.rearrange("(c p) n -> p c n", p=128)
    out_r = out.rearrange("(t p) d -> p t d", p=128)
    ctx_rd = ctx_dram.rearrange("h q p n -> p q h n")

    def kfm(h, a, w):
        return KV2[:, h * 2048 + a: h * 2048 + a + w]

    def vsl(kt, off, w):
        return KV2[:, 8192 + kt * 512 + off: 8192 + kt * 512 + off + w]

    def h2sl(d, a, w):
        return KV2[:, d * 1024 + a: d * 1024 + a + w]

    trst = {"g": 0, "ev": {}}

    def tr_batch(pl, ins, out_ap, eng="dv", pre_pe_ev=None):
        g = trst["g"]
        trst["g"] += 1
        psb = PSB6 if g % 2 == 0 else PSB7
        if g >= 2:
            pl.wait("pe", trst["ev"][g - 2])
        if pre_pe_ev is not None:
            pl.wait("pe", pre_pe_ev)
        ev = None
        n = len(ins)
        for i, src in enumerate(ins):
            fn = (lambda e, o=psb[:, i * 128:(i + 1) * 128], s=src:
                  e.transpose(o, s, IDT[:]))
            ev = pl.op("pe", fn, inc="pe" if i == n - 1 else None)
        pl.wait(eng, ev)
        if eng == "ac":
            evc = pl.op(eng, lambda e, o=out_ap, i=psb[:, 0:128 * n]: e.copy(o, i),
                        inc=eng)
        else:
            evc = pl.op(eng, lambda e, o=out_ap, i=psb[:, 0:128 * n]:
                        e.tensor_copy(o, i), inc=eng)
        trst["ev"][g] = evc
        return evc

    esA = ExitStack()

    def sbA(name, shape, dt):
        return esA.enter_context(nc.sbuf_tensor(name, shape, dt))

    H1 = sbA("H1", [128, NSLOT * 2048], BF16)
    COSF = sbA("COSF", [128, S], BF16)
    SINF = sbA("SINF", [128, S], BF16)
    WQB = sbA("WQB", [128, 16, 512], BF16)
    WKB = sbA("WKB", [128, 16, 512], BF16)
    WVB = sbA("WVB", [128, 16, 512], BF16)
    QF = sbA("QF", [128, 4096], BF16)
    ATT = sbA("ATT", [128, 3 * 2048], BF16)
    AT2 = sbA("AT2", [128, 2048], BF16)
    ESB = sbA("ESB", [128, 2048], F32)
    HTM = sbA("HTM", [128, 4096], BF16)
    TMS = sbA("TMS", [128, 512], BF16)
    TMS3 = sbA("TMS3", [128, 512], BF16)
    TMS4 = sbA("TMS4", [128, 512], BF16)
    CTXS = sbA("CTXS", [128, 256], BF16)

    p0 = Plan(sems, cnt)
    ev_const = None
    for dst, src in ((IDT[:], idt_in), (IDTF[:], idtf_in), (PSW[:], psw_in),
                     (TRI[:], tri_in), (TRI01[:], tri01_in),
                     (ONESB[:], onesb_in), (FLG[:], flg_in), (ONESF[:], onesf_in),
                     (ONESR[:], onesr_in), (COSF[:], cosf_in), (SINF[:], sinf_in),
                     (GWTB[:], gwt_r), (CVEC[:], cvec_in)):
        ev_const = p0.dma("gp", dst, src, "qW")
    p0.wait("gp", ev_const)
    ev_wg0 = ev_const
    for wbuf, wsrc in ((WQB, wq_r), (WKB, wk_r), (WVB, wv_r)):
        ev_wg0 = p0.dma("gp", wbuf[:], wsrc[:, :, 0:512], "qW")

    p0.wait("ac", ev_const)
    p0.wait("dv", ev_const)
    ev_x = [None] * NSLOT
    ev_sq = [None] * NSLOT
    ev_norm = [None] * NSLOT
    ev_h1 = [None] * NSLOT
    for kt in range(NSLOT):
        par = kt % 2
        xt = ESB[:, 0:2048]
        htm = HTM[:, par * 2048:(par + 1) * 2048]
        if kt >= 1:
            p0.wait("sy", ev_sq[kt - 1])
            p0.wait("sy", ev_norm[kt - 1])
        ev_x[kt] = p0.dma("sy", xt, x_r[:, kt, :], "qX")
        p0.wait("dv", ev_x[kt])
        p0.opw("dv", lambda e, i=xt, kt=kt: e.reduce_sum(
            STAT[:, kt:kt + 1], i, axis=AX))
        p0.wait("ac", ev_x[kt])
        ev_sq[kt] = p0.op("ac", lambda e, i=xt, kt=kt: e.activation(
            ATT[:, 0:2048], i, AF.Square, bias=0.0, scale=1.0,
            accum_out=STAT[:, 16 + kt:17 + kt]), inc="ac")
        p0.wait("ac", ev_sq[kt])
        p0.wait("dv", ev_sq[kt])
        p0.opw("dv", lambda e, kt=kt: e.tensor_scalar_mul(
            STAT[:, 32 + kt:33 + kt], STAT[:, kt:kt + 1], -1.0 / D))
        ev_pre = p0.opw("dv", lambda e, kt=kt: e.tensor_scalar_mul(
            STAT[:, 48 + kt:49 + kt], STAT[:, 16 + kt:17 + kt], 1.0 / D))
        p0.wait("ac", ev_pre)
        ev_mu2 = p0.opw("ac", lambda e, kt=kt: e.activation(
            STAT[:, 64 + kt:65 + kt], STAT[:, 32 + kt:33 + kt], AF.Square))
        p0.wait("dv", ev_mu2)
        ev_var = p0.opw("dv", lambda e, kt=kt: e.tensor_sub(
            STAT[:, 64 + kt:65 + kt], STAT[:, 48 + kt:49 + kt],
            STAT[:, 64 + kt:65 + kt]))
        p0.wait("ac", ev_var)
        p0.opw("ac", lambda e, kt=kt: e.activation(
            STAT[:, 64 + kt:65 + kt], STAT[:, 64 + kt:65 + kt], AF.Ln, bias=EPS))
        ev_rstd = p0.opw("ac", lambda e, kt=kt: e.activation(
            STAT[:, 64 + kt:65 + kt], STAT[:, 64 + kt:65 + kt], AF.Exp,
            scale=-0.5))
        p0.wait("dv", ev_rstd)
        if kt >= 2:
            p0.wait("dv", ev_h1[kt - 2])
        ev_norm[kt] = p0.op("dv", lambda e, i=xt, o=htm, kt=kt: e.tensor_scalar(
            out=o, in0=i, scalar1=STAT[:, 32 + kt:33 + kt],
            scalar2=STAT[:, 64 + kt:65 + kt], op0=ALU.add, op1=ALU.mult), inc="dv")
        for half in range(2):
            dst = bass.AP(H1, (half * 8) * 2048 + kt * 128,
                          [[NSLOT * 2048, 128], [2048, 8], [1, 128]])
            ins = [htm[:, (half * 8 + dc) * 128:(half * 8 + dc + 1) * 128]
                   for dc in range(8)]
            ev_h1[kt] = tr_batch(p0, ins, dst, eng="dv",
                                 pre_pe_ev=(ev_norm[kt] if half == 0 else None))
            if kt == 0 and half == 0:
                pass
    if "d_h1" in tap_t:
        p0.wait("sy", ev_h1[NSLOT - 1])
        p0.dma("sy", tap_t["d_h1"], H1[:], "qS")
    if "d_stat" in tap_t:
        p0.wait("sy", ev_h1[NSLOT - 1])
        p0.dma("sy", tap_t["d_stat"], STAT[:], "qS")
    p0.emit(nc)

    ev_spill_all = None
    if stage >= 2:
        p1 = Plan(sems, cnt)
        rs = {"ev_exp_row": None, "ev_ctx_row": None, "ev_ctxev_row": None,
              "pq": []}
        ev_trrow = {}
        ev_spill = {}
        ev_projr = None
        row = 0

        def emit_tail(p1):
            if not rs["pq"]:
                return
            pd = rs["pq"].pop(0)
            h, hh, qt, row_, kn = pd["h"], pd["hh"], pd["qt"], pd["row"], pd["kn"]
            attn, ev_nrm = pd["attn"], pd["ev_nrm"]
            nb = (kn + 7) // 8
            evb = []
            for b_ in range(nb):
                n_t = min(8, kn - b_ * 8)
                ins = [attn[:, (b_ * 8 + t) * 128:(b_ * 8 + t + 1) * 128]
                       for t in range(n_t)]
                eng = "dv"
                if b_ == 0 and rs["ev_ctx_row"] is not None:
                    p1.wait(eng, rs["ev_ctx_row"])
                evb.append(tr_batch(
                    p1, ins, AT2[:, b_ * 1024: b_ * 1024 + n_t * 128],
                    eng=eng, pre_pe_ev=(ev_nrm if b_ == 0 else None)))
            ev_trrow[row_] = ("pe", cnt["pe"])
            p1.wait("pe", rs["ev_ctxev_row"])
            for b_ in range(nb):
                p1.wait("pe", evb[b_])
                n_t = min(8, kn - b_ * 8)
                for t in range(n_t):
                    kt = b_ * 8 + t
                    rs["ev_ctx_row"] = p1.op("pe",
                        lambda e, kt=kt, h=h, kn=kn: e.matmul(
                            PS4[:, 0:128], vsl(kt, h * 128, 128),
                            AT2[:, kt * 128:(kt + 1) * 128],
                            start=(kt == 0), stop=(kt == kn - 1)),
                        inc="pe" if kt == kn - 1 else None)
            cpar = row_ % 2
            p1.wait("dv", rs["ev_ctx_row"])
            if row_ >= 2:
                p1.wait("dv", ev_spill[row_ - 2])
            rs["ev_ctxev_row"] = p1.op("dv", lambda e, cpar=cpar: e.tensor_copy(
                CTXS[:, cpar * 128:(cpar + 1) * 128], PS4[:, 0:128]),
                inc="dv")
            p1.wait("sy", rs["ev_ctxev_row"])
            ev_spill[row_] = p1.dma(
                "sy", ctx_dram[hh, qt, :, :],
                CTXS[:, cpar * 128:(cpar + 1) * 128], "qS")

        for g in range(4):
            if g >= 1:
                p1.wait("gp", ev_projr)
                ev_w = None
                for wbuf, wsrc in ((WQB, wq_r), (WKB, wk_r), (WVB, wv_r)):
                    ev_w = p1.dma("gp", wbuf[:], wsrc[:, :, g * 512:(g + 1) * 512], "qW")
                ev_wcur = ev_w
            else:
                ev_wcur = ev_wg0

            emit_tail(p1)
            emit_tail(p1)

            unit = 0
            last_bank_ev = {0: None, 1: None}

            def bank(u):
                return (PS4, 0) if u % 2 == 0 else (PS5, 1)

            if rs["ev_ctxev_row"] is not None:
                p1.wait("pe", rs["ev_ctxev_row"])
            p1.wait("pe", ev_wcur)
            if rs["ev_ctx_row"] is not None:
                p1.wait("ac", rs["ev_ctx_row"])
                p1.wait("dv", rs["ev_ctx_row"])
            for kt in range(NSLOT):
                pb, bi = bank(unit)
                if last_bank_ev[bi] is not None:
                    p1.wait("pe", last_bank_ev[bi])
                ev_mm = None
                for d_ in range(16):
                    ev_mm = p1.op("pe", lambda e, pb=pb, d=d_, kt=kt: e.matmul(
                        pb[:, 0:512],
                        H1[:, d * 2048 + kt * 128: d * 2048 + kt * 128 + 128],
                        WVB[:, d, :], start=(d == 0), stop=(d == 15)),
                        inc="pe" if d_ == 15 else None)
                p1.wait("ac", ev_mm)
                evv = p1.op("ac", lambda e, pb=pb, kt=kt: e.copy(
                    vsl(kt, 0, 512), pb[:, 0:512]), inc="ac")
                last_bank_ev[bi] = evv
                unit += 1
            ev_t3 = None
            for typ in ("k", "q"):
                wb = WKB if typ == "k" else WQB
                ncc = 4 if typ == "k" else 2
                base = 0 if typ == "k" else 1024
                for h in range(4):
                    for cc in range(ncc):
                        pb, bi = bank(unit)
                        if last_bank_ev[bi] is not None:
                            p1.wait("pe", last_bank_ev[bi])
                        ev_mm = None
                        for d_ in range(16):
                            ev_mm = p1.op("pe",
                                lambda e, pb=pb, d=d_, h=h, cc=cc, wb=wb, base=base: e.matmul(
                                    pb[:, 0:512],
                                    wb[:, d, h * 128:(h + 1) * 128],
                                    H1[:, d * 2048 + base + cc * 512:
                                       d * 2048 + base + (cc + 1) * 512],
                                    start=(d == 0), stop=(d == 15)),
                                inc="pe" if d_ == 15 else None)
                        p1.wait("ac", ev_mm)
                        if ev_t3 is not None:
                            p1.wait("ac", ev_t3)
                        ev_ev = p1.op("ac", lambda e, pb=pb: e.copy(
                            TMS[:], pb[:, 0:512]), inc="ac")
                        p1.wait("dv", ev_ev)
                        ev_t3 = p1.op("dv", lambda e, cc=cc, base=base: e.tensor_mul(
                            TMS3[:], TMS[:],
                            COSF[:, base + cc * 512: base + (cc + 1) * 512]), inc="dv")
                        p1.wait("pe", ev_ev)
                        ev_sw = p1.op("pe", lambda e, pb=pb: e.matmul(
                            pb[:, 0:512], PSW[:], TMS[:], start=True, stop=True),
                            inc="pe")
                        p1.wait("dv", ev_sw)
                        p1.op("dv", lambda e, pb=pb, cc=cc, base=base: e.tensor_mul(
                            TMS4[:], pb[:, 0:512],
                            SINF[:, base + cc * 512: base + (cc + 1) * 512]))
                        if typ == "k":
                            dstap = kfm(h, cc * 512, 512)
                        else:
                            dstap = QF[:, h * 1024 + cc * 512: h * 1024 + (cc + 1) * 512]
                        evr = p1.op("dv", lambda e, o=dstap: e.tensor_add(
                            o, TMS3[:], TMS4[:]), inc="dv")
                        last_bank_ev[bi] = evr
                        unit += 1
            ev_projr = ev_mm

            ev_bias = None
            for h_ in range(4):
                if h_ == 0:
                    p1.wait("dv", rs["ev_ctx_row"])
                ev_q2 = p1.op("dv", lambda e, h=h_: e.tensor_mul(
                    AT2[:, 0:1024], QF[:, h * 1024:(h + 1) * 1024],
                    QF[:, h * 1024:(h + 1) * 1024]), inc="dv")
                p1.wait("pe", ev_q2)
                if h_ == 0:
                    p1.wait("pe", rs["ev_exp_row"])
                ev_qm = None
                for c_ in range(2):
                    ev_qm = p1.op("pe", lambda e, c=c_: e.matmul(
                        PSBIG[0:1, c * 512:(c + 1) * 512], ONESB[:],
                        AT2[:, c * 512:(c + 1) * 512],
                        start=True, stop=True), inc="pe" if c_ == 1 else None)
                p1.wait("dv", ev_qm)
                p1.opw("dv", lambda e: e.reduce_max(
                    STAT[0:1, 92:93], PSBIG[0:1, 0:1024], axis=AX))
                ev_k2 = p1.op("dv", lambda e, h=h_: e.tensor_mul(
                    AT2[:, 0:2048], kfm(h, 0, 2048), kfm(h, 0, 2048)),
                    inc="dv")
                p1.wait("pe", ev_k2)
                ev_km = None
                for c_ in range(4):
                    ev_km = p1.op("pe", lambda e, c=c_: e.matmul(
                        PSBIG[0:1, c * 512:(c + 1) * 512], ONESB[:],
                        AT2[:, c * 512:(c + 1) * 512],
                        start=True, stop=True), inc="pe" if c_ == 3 else None)
                p1.wait("dv", ev_km)
                p1.opw("dv", lambda e: e.reduce_max(
                    STAT[0:1, 93:94], PSBIG[0:1, 0:2048], axis=AX))
                ev_prod = p1.opw("dv", lambda e: e.tensor_mul(
                    STAT[0:1, 92:93], STAT[0:1, 92:93], STAT[0:1, 93:94]))
                p1.wait("ac", ev_prod)
                ev_sqr = p1.opw("ac", lambda e: e.activation(
                    STAT[0:1, 92:93], STAT[0:1, 92:93], AF.Sqrt))
                p1.wait("dv", ev_sqr)
                ev_sc2 = p1.opw("dv", lambda e: e.tensor_scalar_mul(
                    STAT[0:1, 92:93], STAT[0:1, 92:93], -1.02 * INV_SQRT_HD))
                p1.wait("pe", ev_sc2)
                if h_ == 0:
                    p1.wait("pe", rs["ev_ctxev_row"])
                ev_bc = p1.op("pe", lambda e: e.matmul(
                    PS4[:, 0:1], ONESR[:], STAT[0:1, 92:93],
                    start=True, stop=True), inc="pe")
                p1.wait("ac", ev_bc)
                ev_bias = p1.op("ac", lambda e, h=h_: e.copy(
                    STAT[:, 88 + h:89 + h], PS4[:, 0:1]), inc="ac")
            rs["ev_ctxev_row"] = ev_bias

            if g == 3 and "d_kfm" in tap_t:
                p1.wait("sy", last_bank_ev[0])
                p1.wait("sy", last_bank_ev[1])
                p1.dma("sy", tap_t["d_kfm"], KV2[:, 0:8192], "qS")
                p1.dma("sy", tap_t["d_qfm"], QF[:], "qS")
                p1.dma("sy", tap_t["d_vsb"], KV2[:, 8192:16384], "qS")

            for h in range(4):
                hh = g * 4 + h
                for qt in range(NT):
                    col = hh * 8 + qt
                    kn = 9 + qt
                    kctx = kn * 128
                    par = row % 3
                    attn = ATT[:, par * 2048: par * 2048 + 2048]
                    p1.wait("pe", rs["ev_exp_row"])
                    nch = (kctx + 511) // 512
                    ev_sc = None
                    for c in range(nch):
                        w = min(512, kctx - c * 512)
                        ev_sc = p1.op("pe", lambda e, c=c, w=w, h=h, qt=qt: e.matmul(
                            PSBIG[:, c * 512: c * 512 + w],
                            QF[:, h * 1024 + qt * 128: h * 1024 + (qt + 1) * 128],
                            kfm(h, c * 512, w), start=True, stop=True),
                            inc="pe" if c == nch - 1 else None)
                    if len(rs["pq"]) >= 2:
                        emit_tail(p1)
                    p1.wait("ac", ev_sc)
                    if row >= 3:
                        p1.wait("ac", ev_trrow[row - 3])
                    p1.op("ac", lambda e, attn=attn, col=col, h=h: e.activation(
                        attn[:, 0:1024], PSBIG[:, 0:1024], AF.Exp,
                        bias=STAT[:, 88 + h:89 + h], scale=INV_SQRT_HD,
                        accum_out=GRID[:, col:col + 1]))
                    if kn > 9:
                        p1.op("ac", lambda e, attn=attn, col=col, kn=kn, h=h:
                            e.activation(
                                attn[:, 1024:(kn - 1) * 128],
                                PSBIG[:, 1024:(kn - 1) * 128], AF.Exp,
                                bias=STAT[:, 88 + h:89 + h], scale=INV_SQRT_HD,
                                accum_out=GRID[:, 128 + col:129 + col]))
                    ev_exp = p1.op("ac", lambda e, attn=attn, kn=kn, h=h:
                        e.activation(
                            attn[:, (kn - 1) * 128: kn * 128],
                            PSBIG[:, (kn - 1) * 128: kn * 128], AF.Exp,
                            bias=STAT[:, 88 + h:89 + h], scale=INV_SQRT_HD),
                        inc="ac")
                    rs["ev_exp_row"] = ev_exp
                    p1.wait("dv", ev_exp)
                    p1.opw("dv", lambda e, attn=attn, kn=kn: e.tensor_mul(
                        attn[:, (kn - 1) * 128: kn * 128],
                        attn[:, (kn - 1) * 128: kn * 128], TRI01[:]))
                    p1.opw("dv", lambda e, attn=attn, kn=kn: e.reduce_sum(
                        STAT[:, 84:85], attn[:, (kn - 1) * 128: kn * 128],
                        axis=AX))
                    if kn > 9:
                        p1.opw("dv", lambda e, col=col: e.scalar_tensor_tensor(
                            out=STAT[:, 82:83], in0=GRID[:, col:col + 1],
                            scalar=FLG[:, 0:1], in1=GRID[:, 128 + col:129 + col],
                            op0=ALU.mult, op1=ALU.add))
                    else:
                        p1.opw("dv", lambda e, col=col: e.tensor_scalar(
                            out=STAT[:, 82:83], in0=GRID[:, col:col + 1],
                            scalar1=FLG[:, 0:1], scalar2=None, op0=ALU.mult))
                    p1.opw("dv", lambda e: e.tensor_add(
                        STAT[:, 82:83], STAT[:, 82:83], STAT[:, 84:85]))
                    p1.opw("dv", lambda e: e.reciprocal(STAT[:, 83:84], STAT[:, 82:83]))
                    ev_nrm = p1.op("dv", lambda e, attn=attn, kctx=kctx:
                        e.tensor_scalar_mul(
                            attn[:, 0:kctx], attn[:, 0:kctx], STAT[:, 83:84]),
                        inc="dv")
                    rs["pq"].append({"h": h, "hh": hh, "qt": qt, "row": row,
                                     "kn": kn, "attn": attn, "ev_nrm": ev_nrm})
                    row += 1
        emit_tail(p1)
        emit_tail(p1)
        p1.emit(nc)
        ev_spill_all = ("qS", cnt["qS"])

    esA.close()

    ev_x2st = [None] * NT
    if stage >= 3:
        esB = ExitStack()

        def sbB(name, shape, dt):
            return esB.enter_context(nc.sbuf_tensor(name, shape, dt))

        WOB = sbB("WOB", [128, 16, 2048], BF16)
        CTXB = sbB("CTXB", [128, 2, 16, 128], BF16)
        XT = sbB("XT", [128, 2, 2048], F32)
        XNEW = sbB("XNEW", [128, 2048], F32)
        XFM = sbB("XFM", [128, 2048], F32)
        HT2 = sbB("HT2", [128, 2048], BF16)
        SQ2 = sbB("SQ2", [128, 2048], BF16)
        LG = sbB("LG", [128, 64], F32)

        p2 = Plan(sems, cnt)
        ev_wo = None
        for hh in range(16):
            ev_wo = p2.dma("gp", WOB[:, hh, :], wo_r[:, hh, :], "qW")
        ev_sp = None
        for d_ in range(16):
            ev_sp = p2.op("pe", lambda e, d=d_: e.matmul(
                PS5[0:1, 0:8], ONESF[:], GWTB[:, d, :],
                start=(d == 0), stop=(d == 15)), inc="pe" if d_ == 15 else None)
        p2.wait("dv", ev_sp)
        ev_srv = p2.opw("dv", lambda e: e.tensor_copy(SRV[:], PS5[0:1, 0:8]))
        p2.wait("pe", ev_srv)
        ev_sbp = p2.op("pe", lambda e: e.matmul(
            PS5[:, 16:24], ONESR[:], SRV[:], start=True, stop=True), inc="pe")
        p2.wait("dv", ev_sbp)
        ev_sbc = p2.opw("dv", lambda e: e.tensor_copy(SBC[:], PS5[:, 16:24]))

        ev_ctxb = [None, None]
        ev_xt = [None, None]
        ev_wolast = [None, None]
        ev_xtfree = [None, None]
        ev_ep = {0: ev_sbc, 1: ev_sbc}
        ev_gtr = None
        ev_s2prev = None
        ev_xfm_ev = None
        ev_h2w = None
        ev_wm = None
        for qt in range(NT):
            par = qt % 2
            if qt >= 2:
                p2.wait("sy", ev_wolast[par])
                p2.wait("sy", ev_xtfree[par])
            if qt == 0:
                p2.wait("sy", ev_spill_all)
            ev_ctxb[par] = p2.dma("sy", CTXB[:, par, :, :], ctx_rd[:, qt, :, :], "qX")
            ev_xt[par] = p2.dma("sy", XT[:, par, :], x_r[:, 8 + qt, :], "qX")
            for dc in range(4):
                pb = PS4 if dc % 2 == 0 else PS5
                p2.wait("pe", ev_ctxb[par])
                if qt == 0 and dc == 0:
                    p2.wait("pe", ev_wo)
                p2.wait("pe", ev_ep[dc % 2])
                ev_mm = None
                for hh in range(16):
                    ev_mm = p2.op("pe",
                        lambda e, pb=pb, hh=hh, dc=dc, par=par: e.matmul(
                            pb[:, 0:512], CTXB[:, par, hh, :],
                            WOB[:, hh, dc * 512:(dc + 1) * 512],
                            start=(hh == 0), stop=(hh == 15)),
                        inc="pe" if hh == 15 else None)
                if dc == 3:
                    ev_wolast[par] = ev_mm
                p2.wait("dv", ev_mm)
                p2.wait("dv", ev_xt[par])
                if dc == 0 and qt > 0:
                    p2.wait("dv", ev_gtr)
                    p2.wait("dv", ev_s2prev)
                    p2.wait("dv", ev_x2st[qt - 1])
                ev_ep[dc % 2] = p2.op("dv",
                    lambda e, pb=pb, dc=dc, par=par: e.tensor_add(
                        XNEW[:, dc * 512:(dc + 1) * 512], pb[:, 0:512],
                        XT[:, par, dc * 512:(dc + 1) * 512]), inc="dv")
                if dc == 3:
                    ev_xtfree[par] = ev_ep[1]
            ev_xn = ev_ep[1]
            p2.wait("sy", ev_xn)
            ev_x2st[qt] = p2.dma("sy", out_r[:, qt, :], XNEW[:], "qS")
            p2.wait("dv", ev_xn)
            p2.opw("dv", lambda e, qt=qt: e.reduce_sum(
                STAT[:, qt:qt + 1], XNEW[:], axis=AX))
            p2.wait("ac", ev_xn)
            ev_s2 = p2.op("ac", lambda e, qt=qt: e.activation(
                SQ2[:], XNEW[:], AF.Square, bias=0.0, scale=1.0,
                accum_out=STAT[:, 16 + qt:17 + qt]), inc="ac")
            p2.wait("ac", ev_s2)
            ev_s2prev = ev_s2
            p2.wait("dv", ev_s2)
            p2.opw("dv", lambda e, qt=qt: e.tensor_scalar_mul(
                STAT[:, 32 + qt:33 + qt], STAT[:, qt:qt + 1], -1.0 / D))
            ev_pre = p2.opw("dv", lambda e, qt=qt: e.tensor_scalar_mul(
                STAT[:, 48 + qt:49 + qt], STAT[:, 16 + qt:17 + qt], 1.0 / D))
            p2.wait("ac", ev_pre)
            ev_mu2 = p2.opw("ac", lambda e, qt=qt: e.activation(
                STAT[:, 64 + qt:65 + qt], STAT[:, 32 + qt:33 + qt], AF.Square))
            p2.wait("dv", ev_mu2)
            ev_var = p2.opw("dv", lambda e, qt=qt: e.tensor_sub(
                STAT[:, 64 + qt:65 + qt], STAT[:, 48 + qt:49 + qt],
                STAT[:, 64 + qt:65 + qt]))
            p2.wait("ac", ev_var)
            p2.opw("ac", lambda e, qt=qt: e.activation(
                STAT[:, 64 + qt:65 + qt], STAT[:, 64 + qt:65 + qt], AF.Ln, bias=EPS))
            ev_rstd = p2.opw("ac", lambda e, qt=qt: e.activation(
                STAT[:, 64 + qt:65 + qt], STAT[:, 64 + qt:65 + qt], AF.Exp,
                scale=-0.5))
            p2.wait("dv", ev_rstd)
            if ev_h2w is not None:
                p2.wait("dv", ev_h2w)
            ev_n2 = p2.op("dv", lambda e, qt=qt: e.tensor_scalar(
                out=HT2[:], in0=XNEW[:], scalar1=STAT[:, 32 + qt:33 + qt],
                scalar2=STAT[:, 64 + qt:65 + qt], op0=ALU.add, op1=ALU.mult),
                inc="dv")
            for half in range(2):
                dst = bass.AP(KV2, (half * 8) * 1024 + qt * 128,
                              [[16384, 128], [1024, 8], [1, 128]])
                ins = [HT2[:, (half * 8 + dc) * 128:(half * 8 + dc + 1) * 128]
                       for dc in range(8)]
                ev_h2w = tr_batch(p2, ins, dst, eng="dv",
                                  pre_pe_ev=(ev_n2 if half == 0 else None))
            if ev_xfm_ev is not None:
                p2.wait("pe", ev_xfm_ev)
            ev_gt = None
            for dd in range(16):
                ev_gt = p2.op("pe", lambda e, dd=dd: e.transpose(
                    PSBIG[:, dd * 128:(dd + 1) * 128],
                    XNEW[:, dd * 128:(dd + 1) * 128], IDTF[:]),
                    inc="pe" if dd == 15 else None)
            ev_gtr = ev_gt
            p2.wait("dv", ev_gt)
            ev_xfm_ev = p2.op("dv", lambda e: e.tensor_copy(XFM[:], PSBIG[:]),
                              inc="dv")
            p2.wait("pe", ev_ep[0])
            p2.wait("pe", ev_xfm_ev)
            ev_lgm = None
            for d_ in range(16):
                ev_lgm = p2.op("pe", lambda e, d=d_: e.matmul(
                    PS4[:, 0:8], XFM[:, d * 128:(d + 1) * 128], GWTB[:, d, :],
                    start=(d == 0), stop=(d == 15)), inc="pe" if d_ == 15 else None)
            p2.wait("dv", ev_lgm)
            p2.opw("dv", lambda e, qt=qt: e.scalar_tensor_tensor(
                out=LG[:, 0:8], in0=SBC[:], scalar=STAT[:, 32 + qt:33 + qt],
                in1=PS4[:, 0:8], op0=ALU.mult, op1=ALU.add))
            ev_lg = p2.opw("dv", lambda e, qt=qt: e.tensor_scalar_mul(
                LG[:, 0:8], LG[:, 0:8], STAT[:, 64 + qt:65 + qt]))
            ev_ep[0] = ev_lg
            ev_rm = p2.opw("dv", lambda e: e.reduce_max(
                STAT[:, 84:85], LG[:, 0:8], axis=AX, negate=True))
            p2.wait("ac", ev_rm)
            ev_sm = p2.op("ac", lambda e: e.activation(
                LG[:, 8:16], LG[:, 0:8], AF.Exp, bias=STAT[:, 84:85],
                accum_out=STAT[:, 85:86]), inc="ac")
            p2.wait("ac", ev_sm)
            p2.wait("dv", ev_sm)
            p2.opw("dv", lambda e: e.reciprocal(STAT[:, 86:87], STAT[:, 85:86]))
            p2.opw("dv", lambda e: e.tensor_scalar_mul(
                LG[:, 8:16], LG[:, 8:16], STAT[:, 86:87]))
            sc = LG[:, 8:16].rearrange("p (a s) -> p a s", s=2)
            p2.opw("dv", lambda e, sc=sc: e.tensor_max(
                LG[:, 16:20], sc[:, :, 0], sc[:, :, 1]))
            p2.opw("dv", lambda e: e.reduce_max(STAT[:, 87:88], LG[:, 16:20], axis=AX))
            p2.opw("dv", lambda e: e.tensor_scalar(
                out=LG[:, 20:24], in0=LG[:, 16:20], scalar1=STAT[:, 87:88],
                scalar2=None, op0=ALU.is_equal))
            p2.opw("dv", lambda e: e.scalar_tensor_tensor(
                out=LG[:, 24:28], in0=LG[:, 20:24], scalar=-1.0e30,
                in1=LG[:, 16:20], op0=ALU.mult, op1=ALU.add))
            p2.opw("dv", lambda e: e.reduce_max(STAT[:, 88:89], LG[:, 24:28], axis=AX))
            p2.opw("dv", lambda e: e.tensor_scalar(
                out=LG[:, 20:24], in0=LG[:, 16:20], scalar1=STAT[:, 88:89],
                scalar2=None, op0=ALU.is_ge))
            mk = LG[:, 32:40].rearrange("p (a s) -> p a s", s=2)
            p2.opw("dv", lambda e, sc=sc, mk=mk: e.tensor_mul(
                mk[:, :, 0], sc[:, :, 0], LG[:, 20:24]))
            p2.opw("dv", lambda e, sc=sc, mk=mk: e.tensor_mul(
                mk[:, :, 1], sc[:, :, 1], LG[:, 20:24]))
            p2.opw("dv", lambda e: e.reduce_max(STAT[:, 87:88], LG[:, 32:40], axis=AX))
            p2.opw("dv", lambda e: e.tensor_scalar(
                out=LG[:, 40:48], in0=LG[:, 32:40], scalar1=STAT[:, 87:88],
                scalar2=None, op0=ALU.is_equal))
            p2.opw("dv", lambda e: e.scalar_tensor_tensor(
                out=LG[:, 48:56], in0=LG[:, 40:48], scalar=-1.0e30,
                in1=LG[:, 32:40], op0=ALU.mult, op1=ALU.add))
            p2.opw("dv", lambda e: e.reduce_max(STAT[:, 88:89], LG[:, 48:56], axis=AX))
            p2.opw("dv", lambda e: e.tensor_scalar(
                out=LG[:, 40:48], in0=LG[:, 32:40], scalar1=STAT[:, 88:89],
                scalar2=None, op0=ALU.is_ge))
            ev_wm = p2.opw("dv", lambda e, qt=qt: e.tensor_mul(
                WMAT[:, qt * 8:(qt + 1) * 8], LG[:, 32:40], LG[:, 40:48]))
        if "d_h2" in tap_t:
            p2.wait("sy", ev_h2w)
            p2.dma("sy", tap_t["d_h2"], KV2[:], "qS")
        if "d_wmat" in tap_t:
            p2.wait("sy", ev_wm)
            p2.dma("sy", tap_t["d_wmat"], WMAT[:], "qS")
        if "d_logit" in tap_t:
            p2.wait("sy", ev_wm)
            p2.dma("sy", tap_t["d_logit"], WMAT[:], "qS")
        p2.emit(nc)
        esB.close()

    if stage >= 4:
        esC = ExitStack()

        def sbC(name, shape, dt):
            return esC.enter_context(nc.sbuf_tensor(name, shape, dt))

        GUB = sbC("GUB", [128, 3, 16, 384], BF16)
        UUB = sbC("UUB", [128, 3, 16, 384], BF16)
        DOWNB = sbC("DOWNB", [128, 11, 2048], BF16)
        ACTA = sbC("ACTA", [128, NT, FE], BF16)
        ACTT = sbC("ACTT", [128, 2, FE], BF16)
        SGB = sbC("SGB", [128, 512], BF16)
        ESB3 = sbC("ESB3", [128, 2048], F32)

        p3 = Plan(sems, cnt)
        FCS = ((0, 384), (384, 384), (768, 384), (1152, 256))
        experts = [("sh", 0), ("sh", 1)] + [("e", n) for n in range(NE)]
        gfc = 0
        ev_gu_cons = {}
        ev_down_done = None
        ev_tr_exp = None
        ev_store = {}
        ev_dn_tt = {}
        ev_evac_h = [None, None]
        ev_silu = None
        ev_stt = None
        for mi, (kind, idx) in enumerate(experts):
            def gu_src(off, w, kind=kind, idx=idx):
                if kind == "sh":
                    return (shg_r[:, :, idx * FE + off: idx * FE + off + w],
                            shu_r[:, :, idx * FE + off: idx * FE + off + w])
                return (gate_t[idx].rearrange("(c p) f -> p c f", p=128)[
                            :, :, off:off + w],
                        up_t[idx].rearrange("(c p) f -> p c f", p=128)[
                            :, :, off:off + w])

            def emit_gu_load(gfc_, off, w):
                gpar = gfc_ % 3
                p3.wait("gp", ev_gu_cons.get(gfc_ - 3))
                gsrc, usrc = gu_src(off, w)
                p3.dma("gp", GUB[:, gpar, :, 0:w], gsrc, "qW")
                return p3.dma("gp", UUB[:, gpar, :, 0:w], usrc, "qW")

            myfc = []
            for fci, (off, w) in enumerate(FCS):
                gpar = gfc % 3
                evu = emit_gu_load(gfc, off, w) if fci < 3 else None
                myfc.append([gfc, gpar, off, w, evu])
                gfc += 1
            p3.wait("gp", ev_down_done)
            ev_db = None
            for fc in range(11):
                if kind == "sh":
                    dsrc = shd_t[idx * FE + fc * 128: idx * FE + (fc + 1) * 128, :]
                else:
                    dsrc = down_t[idx][fc * 128:(fc + 1) * 128, :]
                ev_db = p3.dma("gp", DOWNB[:, fc, :], dsrc, "qW")
            for fci, rec in enumerate(myfc):
                (gfci, gpar, off, w, evu) = rec
                for tt in range(NT):
                    p3.wait("pe", evu)
                    if ev_silu is not None:
                        p3.wait("pe", ev_silu)
                    ev_g = None
                    for d_ in range(16):
                        ev_g = p3.op("pe",
                            lambda e, d=d_, tt=tt, gpar=gpar, w=w: e.matmul(
                                PS4[:, 0:w], h2sl(d, tt * 128, 128),
                                GUB[:, gpar, d, 0:w],
                                start=(d == 0), stop=(d == 15)),
                            inc="pe" if d_ == 15 else None)
                    p3.wait("ac", ev_g)
                    if ev_stt is not None:
                        p3.wait("ac", ev_stt)
                    ev_silu = p3.op("ac", lambda e, w=w: e.activation(
                        SGB[:, 0:w], PS4[:, 0:w], AF.Silu), inc="ac")
                    if ev_stt is not None:
                        p3.wait("pe", ev_stt)
                    ev_u = None
                    for d_ in range(16):
                        ev_u = p3.op("pe",
                            lambda e, d=d_, tt=tt, gpar=gpar, w=w: e.matmul(
                                PS5[:, 0:w], h2sl(d, tt * 128, 128),
                                UUB[:, gpar, d, 0:w],
                                start=(d == 0), stop=(d == 15)),
                            inc="pe" if d_ == 15 else None)
                    p3.wait("dv", ev_u)
                    p3.wait("dv", ev_silu)
                    if ev_tr_exp is not None:
                        p3.wait("dv", ev_tr_exp)
                    if kind == "sh":
                        ev_stt = p3.op("dv",
                            lambda e, tt=tt, off=off, w=w: e.scalar_tensor_tensor(
                                out=ACTA[:, tt, off:off + w], in0=PS5[:, 0:w],
                                scalar=1.0, in1=SGB[:, 0:w],
                                op0=ALU.mult, op1=ALU.mult), inc="dv")
                    else:
                        ev_stt = p3.op("dv",
                            lambda e, tt=tt, off=off, w=w, n=idx:
                            e.scalar_tensor_tensor(
                                out=ACTA[:, tt, off:off + w], in0=PS5[:, 0:w],
                                scalar=WMAT[:, tt * 8 + n: tt * 8 + n + 1],
                                in1=SGB[:, 0:w], op0=ALU.mult, op1=ALU.mult),
                            inc="dv")
                ev_gu_cons[gfci] = ev_stt
                if fci == 0:
                    off2, w2 = FCS[3]
                    myfc[3][4] = emit_gu_load(myfc[3][0], off2, w2)
            for tt in range(NT):
                gtt = mi * NT + tt
                apar = gtt % 2
                p3.wait("dv", ev_dn_tt.get(gtt - 2))
                p3.wait("ac", ev_dn_tt.get(gtt - 2))
                ev_t1 = tr_batch(p3, [ACTA[:, tt, c * 128:(c + 1) * 128]
                                      for c in range(8)],
                                 ACTT[:, apar, 0:1024], eng="dv",
                                 pre_pe_ev=(ev_stt if tt == 0 else None))
                ev_t2 = tr_batch(p3, [ACTA[:, tt, c * 128:(c + 1) * 128]
                                      for c in range(8, 11)],
                                 ACTT[:, apar, 1024:1408], eng="ac")
                if tt == NT - 1:
                    ev_tr_exp = ("pe", cnt["pe"])
                p3.wait("pe", ev_db)
                p3.wait("pe", ev_t1)
                p3.wait("pe", ev_t2)
                for half in range(2):
                    p3.wait("pe", ev_evac_h[half])
                    ev_dn = None
                    for fc in range(11):
                        for dc in range(2):
                            co = half * 1024 + dc * 512
                            ev_dn = p3.op("pe", lambda e, fc=fc, co=co, apar=apar:
                                e.matmul(
                                    PSBIG[:, co: co + 512],
                                    ACTT[:, apar, fc * 128:(fc + 1) * 128],
                                    DOWNB[:, fc, co: co + 512],
                                    start=(fc == 0), stop=(fc == 10)),
                                inc="pe" if (fc == 10 and dc == 1) else None)
                    eng = "dv" if half == 0 else "ac"
                    p3.wait(eng, ev_dn)
                    p3.wait(eng, ev_store.get((tt, half)))
                    hsl = slice(half * 1024, (half + 1) * 1024)
                    if eng == "dv":
                        ev_evac_h[half] = p3.op(eng, lambda e, hsl=hsl:
                            e.tensor_copy(ESB3[:, hsl], PSBIG[:, hsl]), inc=eng)
                    else:
                        ev_evac_h[half] = p3.op(eng, lambda e, hsl=hsl:
                            e.copy(ESB3[:, hsl], PSBIG[:, hsl]), inc=eng)
                    p3.wait("gp", ev_evac_h[half])
                    p3.wait("gp", ev_store.get((tt, half)))
                    if mi == 0:
                        p3.wait("gp", ev_x2st[tt])
                    ev_store[(tt, half)] = p3.dma(
                        "gp", out_r[:, tt, hsl], ESB3[:, hsl], "qA", accum=True)
                ev_dn_tt[gtt] = ev_dn
                ev_down_done = ev_dn
        if "d_acta" in tap_t:
            p3.wait("sy", ev_stt)
            p3.dma("sy", tap_t["d_acta"], ACTA[:], "qS")
        p3.op("gp", lambda e: e.engine_nop())
        p3.wait("gp", ("qA", cnt["qA"]))
        p3.emit(nc)
        esC.close()

    esP.close()
    return nc, tap_t


def _rope_tables(j):
    if j == 1:
        pos = np.arange(S, dtype=np.float32)
    else:
        pos = np.concatenate([np.zeros(1024, np.float32),
                              np.arange(1024, dtype=np.float32)])
    freqs = 1.0 / 10000.0 ** (np.arange(0, HD, 2, dtype=np.float32) / HD)
    ang = np.outer(pos, freqs)
    c = np.cos(ang).T.astype(np.float32)
    s_ = np.sin(ang).T.astype(np.float32)
    cosf = np.concatenate([c, c], axis=0)
    sinf = np.concatenate([-s_, s_], axis=0)
    return np.ascontiguousarray(cosf), np.ascontiguousarray(sinf)


def stage_inputs(inputs):
    x = np.asarray(inputs["L0_attn_x"], np.float32)
    perm = np.concatenate([np.arange(0, HD, 2), np.arange(1, HD, 2)])
    permf = np.concatenate([h * HD + perm for h in range(H)])
    wq_p = np.ascontiguousarray(np.asarray(inputs["L0_attn_Wq"], np.float32)[:, permf])
    wk_p = np.ascontiguousarray(np.asarray(inputs["L0_attn_Wk"], np.float32)[:, permf])
    wv = np.ascontiguousarray(np.asarray(inputs["L0_attn_Wv"], np.float32))
    wo = np.ascontiguousarray(np.asarray(inputs["L0_attn_Wo"], np.float32))
    gwt = np.ascontiguousarray(np.asarray(inputs["L0_ffn_gate_weight"], np.float32).T)
    gate_t = np.ascontiguousarray(
        np.asarray(inputs["L0_ffn_gate_w"], np.float32).transpose(0, 2, 1))
    up_t = np.ascontiguousarray(
        np.asarray(inputs["L0_ffn_up_w"], np.float32).transpose(0, 2, 1))
    down_t = np.ascontiguousarray(
        np.asarray(inputs["L0_ffn_down_w"], np.float32).transpose(0, 2, 1))
    shg_t = np.ascontiguousarray(np.asarray(inputs["L0_ffn_sh_gate_w"], np.float32).T)
    shu_t = np.ascontiguousarray(np.asarray(inputs["L0_ffn_sh_up_w"], np.float32).T)
    shd_t = np.ascontiguousarray(np.asarray(inputs["L0_ffn_sh_down_w"], np.float32).T)

    idt = np.eye(128, dtype=np.float32).astype(ml_dtypes.bfloat16)
    idtf = np.eye(128, dtype=np.float32)
    pswap = np.zeros((128, 128), np.float32)
    for i in range(64):
        pswap[i, i + 64] = 1.0
        pswap[i + 64, i] = 1.0
    pswap = pswap.astype(ml_dtypes.bfloat16)
    tri = np.where(np.arange(128)[None, :] > np.arange(128)[:, None],
                   np.float32(-1.0e30), np.float32(0.0)).astype(np.float32)
    tri01 = np.where(np.arange(128)[None, :] > np.arange(128)[:, None],
                     np.float32(0.0), np.float32(1.0)).astype(ml_dtypes.bfloat16)
    onesb = np.ones((128, 1), ml_dtypes.bfloat16)
    onesf = np.ones((128, 1), np.float32)
    onesr = np.ones((1, 128), np.float32)

    in_maps = []
    for c in range(8):
        b, j = divmod(c, 2)
        xs = np.zeros((S, D), np.float32)
        if j == 1:
            xs[:] = x[b]
        else:
            xs[1024:] = x[b, 0:1024]
        cosf, sinf = _rope_tables(j)
        flg = np.full((128, 1), float(j), np.float32)
        in_maps.append({
            "x": xs, "wq": wq_p, "wk": wk_p, "wv": wv, "wo": wo, "gwt": gwt,
            "gate_t": gate_t, "up_t": up_t, "down_t": down_t,
            "shg_t": shg_t, "shu_t": shu_t, "shd_t": shd_t,
            "cosf": cosf, "sinf": sinf, "idt": idt, "idtf": idtf,
            "psw": pswap, "tri": tri, "tri01": tri01, "onesb": onesb,
            "flg": flg,
            "onesf": onesf, "onesr": onesr,
            "cvec": np.broadcast_to(
                np.asarray(CONST_VALS, np.float32)[None, :],
                (128, len(CONST_VALS))).copy(),
        })
    return in_maps


def kernel(**inputs):
    in_maps = stage_inputs(inputs)
    nc, _ = build(stage=4)
    res = run_bass_kernel_spmd(nc, in_maps, core_ids=list(range(8)))
    outf = np.empty((B, S, D), np.float32)
    for c in range(8):
        b, j = divmod(c, 2)
        outf[b, j * 1024:(j + 1) * 1024] = res.results[c]["out"]
    return outf


if __name__ == "__main__":
    import sys
    st = int(sys.argv[1]) if len(sys.argv) > 1 else 4
    nc, _ = build(stage=st)
    print("built stage", st, "instructions:", nc._state.n_instructions())


# version 16
# speedup vs baseline: 1.0988x; 1.0988x over previous
import numpy as np
import ml_dtypes
from contextlib import ExitStack

import concourse.bass as bass
import concourse.mybir as mybir
from concourse.bass_utils import run_bass_kernel_spmd

F32 = mybir.dt.float32
BF16 = mybir.dt.bfloat16
AF = mybir.ActivationFunctionType
ALU = mybir.AluOpType
AX = mybir.AxisListType.X

B, S, D = 4, 2048, 2048
H, HD = 16, 128
NSLOT = 16
NT = 8
FE = 1408
FS = 2816
NE = 8
INV_SQRT_HD = 1.0 / float(np.sqrt(HD))
EPS = 1e-5
CONST_VALS = [EPS, -0.5, INV_SQRT_HD, -INV_SQRT_HD, 1.0 / D, -1.0 / D,
              -1.0e30, 1.0, 0.0, 2.0]


class Plan:
    ENGINES = ("pe", "dv", "ac", "gp", "sy")

    def __init__(self, sems, cnt):
        self.q = {e: [] for e in self.ENGINES}
        self.sems = sems
        self.cnt = cnt

    def op(self, eng, fn, inc=None, incn=1):
        ev = None
        if inc is not None:
            self.cnt[inc] += incn
            ev = (inc, self.cnt[inc])
        self.q[eng].append(("op", fn, inc, incn))
        return ev

    def wait(self, eng, ev):
        if ev is None:
            return
        self.q[eng].append(("wait", ev[0], ev[1]))

    def opw(self, eng, fn):
        ev = self.op(eng, fn, inc=eng)
        self.wait(eng, ev)
        return ev

    def dma(self, eng, out_ap, in_ap, inc, accum=False):
        def fn(e, o=out_ap, i=in_ap, a=accum):
            if a:
                return e.dma_start(o, i, accum_op=ALU.add)
            return e.dma_start(o, i)
        return self.op(eng, fn, inc=inc, incn=16)

    def replay(self, eng_name, e):
        for item in self.q[eng_name]:
            if item[0] == "op":
                _, fn, inc, incn = item
                inst = fn(e)
                if inc is not None:
                    inst.then_inc(self.sems[inc], incn)
            else:
                _, sem, val = item
                e.wait_ge(self.sems[sem], val)

    def emit(self, nc):
        with nc.Block() as blk:
            if self.q["gp"]:
                blk.gpsimd(lambda e: self.replay("gp", e))
            if self.q["sy"]:
                blk.sync(lambda e: self.replay("sy", e))
            if self.q["pe"]:
                blk.tensor(lambda e: self.replay("pe", e))
            if self.q["dv"]:
                blk.vector(lambda e: self.replay("dv", e))
            if self.q["ac"]:
                blk.scalar(lambda e: self.replay("ac", e))


def build(stage=4, taps=()):
    nc = bass.Bass("TRN2", target_bir_lowering=False)

    def din(name, shape, dt=F32):
        return nc.dram_tensor(name, shape, dt, kind="ExternalInput").ap()

    x_in = din("x", [S, D])
    wq = din("wq", [D, D])
    wk = din("wk", [D, D])
    wv = din("wv", [D, D])
    wo = din("wo", [D, D])
    gwt = din("gwt", [D, NE])
    gate_t = din("gate_t", [NE, D, FE])
    up_t = din("up_t", [NE, D, FE])
    down_t = din("down_t", [NE, FE, D])
    shg_t = din("shg_t", [D, FS])
    shu_t = din("shu_t", [D, FS])
    shd_t = din("shd_t", [FS, D])
    cosf_in = din("cosf", [128, S])
    sinf_in = din("sinf", [128, S])
    idt_in = din("idt", [128, 128], BF16)
    idtf_in = din("idtf", [128, 128])
    psw_in = din("psw", [128, 128], BF16)
    tri_in = din("tri", [128, 128])
    tri01_in = din("tri01", [128, 128], BF16)
    onesb_in = din("onesb", [128, 1], BF16)
    flg_in = din("flg", [128, 1])
    onesf_in = din("onesf", [128, 1])
    onesr_in = din("onesr", [1, 128])
    cvec_in = din("cvec", [128, len(CONST_VALS)])

    out = nc.dram_tensor("out", [1024, D], F32, kind="ExternalOutput").ap()
    tk = lambda n: "ExternalOutput" if n in taps else "Internal"
    ctx_dram = nc.dram_tensor("d_ctx", [H, NT, 128, 128], BF16, kind=tk("d_ctx")).ap()
    tap_t = {}
    for nm, shp, dt in (("d_h1", [128, NSLOT * 2048], BF16),
                        ("d_kfm", [128, 4 * 2048], BF16),
                        ("d_qfm", [128, 4 * 1024], BF16),
                        ("d_vsb", [128, NSLOT * 512], BF16),
                        ("d_h2", [128, NSLOT * 1024], BF16),
                        ("d_wmat", [128, NT * NE], F32),
                        ("d_stat", [128, 96], F32),
                        ("d_acta", [128, NT * FE], BF16),
                        ("d_logit", [128, NT * NE], F32)):
        if nm in taps:
            tap_t[nm] = nc.dram_tensor(nm, shp, dt, kind="ExternalOutput").ap()

    esP = ExitStack()

    def sbP(name, shape, dt):
        return esP.enter_context(nc.sbuf_tensor(name, shape, dt))

    def psP(name, shape, dt=F32):
        return esP.enter_context(nc.psum_tensor(name, shape, dt))

    IDT = sbP("IDT", [128, 128], BF16)
    IDTF = sbP("IDTF", [128, 128], F32)
    PSW = sbP("PSW", [128, 128], BF16)
    TRI = sbP("TRI", [128, 128], F32)
    TRI01 = sbP("TRI01", [128, 128], BF16)
    ONESB = sbP("ONESB", [128, 1], BF16)
    FLG = sbP("FLG", [128, 1], F32)
    ONESF = sbP("ONESF", [128, 1], F32)
    ONESR = sbP("ONESR", [1, 128], F32)
    GWTB = sbP("GWTB", [128, 16, NE], F32)
    KV2 = sbP("KV2", [128, 16384], BF16)
    GRID = sbP("GRID", [128, 256], F32)
    STAT = sbP("STAT", [128, 96], F32)
    WMAT = sbP("WMAT", [128, NT * NE], F32)
    SBC = sbP("SBC", [128, NE], F32)
    SRV = sbP("SRV", [1, NE], F32)
    CVEC = sbP("CVEC", [128, len(CONST_VALS)], F32)
    for ci, cv in enumerate(CONST_VALS):
        nc.const_aps.register(F32, cv, CVEC[:, ci:ci + 1])

    PSBIG = psP("PSBIG", [128, 2048])
    PS4 = psP("PS4", [128, 512])
    PS5 = psP("PS5", [128, 512])
    PSB6 = psP("PSB6", [128, 1024], BF16)
    PSB7 = psP("PSB7", [128, 1024], BF16)

    sems = {}
    cnt = {}
    for s_ in ("qW", "qX", "qS", "qA", "pe", "dv", "ac"):
        sems[s_] = esP.enter_context(nc.semaphore(s_))
        cnt[s_] = 0

    x_r = x_in.rearrange("(t p) d -> p t d", p=128)
    wq_r = wq.rearrange("(c p) n -> p c n", p=128)
    wk_r = wk.rearrange("(c p) n -> p c n", p=128)
    wv_r = wv.rearrange("(c p) n -> p c n", p=128)
    wo_r = wo.rearrange("(c p) n -> p c n", p=128)
    gwt_r = gwt.rearrange("(c p) n -> p c n", p=128)
    shg_r = shg_t.rearrange("(c p) n -> p c n", p=128)
    shu_r = shu_t.rearrange("(c p) n -> p c n", p=128)
    out_r = out.rearrange("(t p) d -> p t d", p=128)
    ctx_rd = ctx_dram.rearrange("h q p n -> p q h n")

    def kfm(h, a, w):
        return KV2[:, h * 2048 + a: h * 2048 + a + w]

    def vsl(kt, off, w):
        return KV2[:, 8192 + kt * 512 + off: 8192 + kt * 512 + off + w]

    def h2sl(d, a, w):
        return KV2[:, d * 1024 + a: d * 1024 + a + w]

    trst = {"g": 0, "ev": {}}

    def tr_batch(pl, ins, out_ap, eng="dv", pre_pe_ev=None):
        g = trst["g"]
        trst["g"] += 1
        psb = PSB6 if g % 2 == 0 else PSB7
        if g >= 2:
            pl.wait("pe", trst["ev"][g - 2])
        if pre_pe_ev is not None:
            pl.wait("pe", pre_pe_ev)
        ev = None
        n = len(ins)
        for i, src in enumerate(ins):
            fn = (lambda e, o=psb[:, i * 128:(i + 1) * 128], s=src:
                  e.transpose(o, s, IDT[:]))
            ev = pl.op("pe", fn, inc="pe" if i == n - 1 else None)
        pl.wait(eng, ev)
        if eng == "ac":
            evc = pl.op(eng, lambda e, o=out_ap, i=psb[:, 0:128 * n]: e.copy(o, i),
                        inc=eng)
        else:
            evc = pl.op(eng, lambda e, o=out_ap, i=psb[:, 0:128 * n]:
                        e.tensor_copy(o, i), inc=eng)
        trst["ev"][g] = evc
        return evc

    esA = ExitStack()

    def sbA(name, shape, dt):
        return esA.enter_context(nc.sbuf_tensor(name, shape, dt))

    H1 = sbA("H1", [128, NSLOT * 2048], BF16)
    COSF = sbA("COSF", [128, S], BF16)
    SINF = sbA("SINF", [128, S], BF16)
    WQB = sbA("WQB", [128, 16, 512], BF16)
    WKB = sbA("WKB", [128, 16, 512], BF16)
    WVB = sbA("WVB", [128, 16, 512], BF16)
    QF = sbA("QF", [128, 4096], BF16)
    ATT = sbA("ATT", [128, 3 * 2048], BF16)
    AT2 = sbA("AT2", [128, 2048], BF16)
    ESB = sbA("ESB", [128, 2048], F32)
    HTM = sbA("HTM", [128, 4096], BF16)
    TMS = sbA("TMS", [128, 512], BF16)
    TMS3 = sbA("TMS3", [128, 512], BF16)
    TMS4 = sbA("TMS4", [128, 512], BF16)
    CTXS = sbA("CTXS", [128, 256], BF16)

    p0 = Plan(sems, cnt)
    ev_const = None
    for dst, src in ((IDT[:], idt_in), (IDTF[:], idtf_in), (PSW[:], psw_in),
                     (TRI[:], tri_in), (TRI01[:], tri01_in),
                     (ONESB[:], onesb_in), (FLG[:], flg_in), (ONESF[:], onesf_in),
                     (ONESR[:], onesr_in), (COSF[:], cosf_in), (SINF[:], sinf_in),
                     (GWTB[:], gwt_r), (CVEC[:], cvec_in)):
        ev_const = p0.dma("gp", dst, src, "qW")
    p0.wait("gp", ev_const)
    ev_wg0 = ev_const
    for wbuf, wsrc in ((WQB, wq_r), (WKB, wk_r), (WVB, wv_r)):
        ev_wg0 = p0.dma("gp", wbuf[:], wsrc[:, :, 0:512], "qW")

    p0.wait("ac", ev_const)
    p0.wait("dv", ev_const)
    ev_x = [None] * NSLOT
    ev_sq = [None] * NSLOT
    ev_norm = [None] * NSLOT
    ev_h1 = [None] * NSLOT
    for kt in range(NSLOT):
        par = kt % 2
        xt = ESB[:, 0:2048]
        htm = HTM[:, par * 2048:(par + 1) * 2048]
        if kt >= 1:
            p0.wait("sy", ev_sq[kt - 1])
            p0.wait("sy", ev_norm[kt - 1])
        ev_x[kt] = p0.dma("sy", xt, x_r[:, kt, :], "qX")
        p0.wait("dv", ev_x[kt])
        p0.opw("dv", lambda e, i=xt, kt=kt: e.reduce_sum(
            STAT[:, kt:kt + 1], i, axis=AX))
        p0.wait("ac", ev_x[kt])
        ev_sq[kt] = p0.op("ac", lambda e, i=xt, kt=kt: e.activation(
            ATT[:, 0:2048], i, AF.Square, bias=0.0, scale=1.0,
            accum_out=STAT[:, 16 + kt:17 + kt]), inc="ac")
        p0.wait("ac", ev_sq[kt])
        p0.wait("dv", ev_sq[kt])
        p0.opw("dv", lambda e, kt=kt: e.tensor_scalar_mul(
            STAT[:, 32 + kt:33 + kt], STAT[:, kt:kt + 1], -1.0 / D))
        ev_pre = p0.opw("dv", lambda e, kt=kt: e.tensor_scalar_mul(
            STAT[:, 48 + kt:49 + kt], STAT[:, 16 + kt:17 + kt], 1.0 / D))
        p0.wait("ac", ev_pre)
        ev_mu2 = p0.opw("ac", lambda e, kt=kt: e.activation(
            STAT[:, 64 + kt:65 + kt], STAT[:, 32 + kt:33 + kt], AF.Square))
        p0.wait("dv", ev_mu2)
        ev_var = p0.opw("dv", lambda e, kt=kt: e.tensor_sub(
            STAT[:, 64 + kt:65 + kt], STAT[:, 48 + kt:49 + kt],
            STAT[:, 64 + kt:65 + kt]))
        p0.wait("ac", ev_var)
        p0.opw("ac", lambda e, kt=kt: e.activation(
            STAT[:, 64 + kt:65 + kt], STAT[:, 64 + kt:65 + kt], AF.Ln, bias=EPS))
        ev_rstd = p0.opw("ac", lambda e, kt=kt: e.activation(
            STAT[:, 64 + kt:65 + kt], STAT[:, 64 + kt:65 + kt], AF.Exp,
            scale=-0.5))
        p0.wait("dv", ev_rstd)
        if kt >= 2:
            p0.wait("dv", ev_h1[kt - 2])
        ev_norm[kt] = p0.op("dv", lambda e, i=xt, o=htm, kt=kt: e.tensor_scalar(
            out=o, in0=i, scalar1=STAT[:, 32 + kt:33 + kt],
            scalar2=STAT[:, 64 + kt:65 + kt], op0=ALU.add, op1=ALU.mult), inc="dv")
        for half in range(2):
            dst = bass.AP(H1, (half * 8) * 2048 + kt * 128,
                          [[NSLOT * 2048, 128], [2048, 8], [1, 128]])
            ins = [htm[:, (half * 8 + dc) * 128:(half * 8 + dc + 1) * 128]
                   for dc in range(8)]
            ev_h1[kt] = tr_batch(p0, ins, dst, eng="dv",
                                 pre_pe_ev=(ev_norm[kt] if half == 0 else None))
            if kt == 0 and half == 0:
                pass
    if "d_h1" in tap_t:
        p0.wait("sy", ev_h1[NSLOT - 1])
        p0.dma("sy", tap_t["d_h1"], H1[:], "qS")
    if "d_stat" in tap_t:
        p0.wait("sy", ev_h1[NSLOT - 1])
        p0.dma("sy", tap_t["d_stat"], STAT[:], "qS")
    p0.emit(nc)

    ev_spill_all = None
    if stage >= 2:
        p1 = Plan(sems, cnt)
        rs = {"ev_exp_row": None, "ev_ctx_row": None, "ev_ctxev_row": None,
              "pq": []}
        ev_trrow = {}
        ev_spill = {}
        ev_projr = None
        row = 0

        def emit_tail(p1):
            if not rs["pq"]:
                return
            pd = rs["pq"].pop(0)
            h, hh, qt, row_, kn = pd["h"], pd["hh"], pd["qt"], pd["row"], pd["kn"]
            attn, ev_nrm = pd["attn"], pd["ev_nrm"]
            nb = (kn + 7) // 8
            evb = []
            for b_ in range(nb):
                n_t = min(8, kn - b_ * 8)
                ins = [attn[:, (b_ * 8 + t) * 128:(b_ * 8 + t + 1) * 128]
                       for t in range(n_t)]
                eng = "dv"
                if b_ == 0 and rs["ev_ctx_row"] is not None:
                    p1.wait(eng, rs["ev_ctx_row"])
                evb.append(tr_batch(
                    p1, ins, AT2[:, b_ * 1024: b_ * 1024 + n_t * 128],
                    eng=eng, pre_pe_ev=(ev_nrm if b_ == 0 else None)))
            ev_trrow[row_] = ("pe", cnt["pe"])
            p1.wait("pe", rs["ev_ctxev_row"])
            for b_ in range(nb):
                p1.wait("pe", evb[b_])
                n_t = min(8, kn - b_ * 8)
                for t in range(n_t):
                    kt = b_ * 8 + t
                    rs["ev_ctx_row"] = p1.op("pe",
                        lambda e, kt=kt, h=h, kn=kn: e.matmul(
                            PS4[:, 0:128], vsl(kt, h * 128, 128),
                            AT2[:, kt * 128:(kt + 1) * 128],
                            start=(kt == 0), stop=(kt == kn - 1)),
                        inc="pe" if kt == kn - 1 else None)
            cpar = row_ % 2
            p1.wait("dv", rs["ev_ctx_row"])
            if row_ >= 2:
                p1.wait("dv", ev_spill[row_ - 2])
            rs["ev_ctxev_row"] = p1.op("dv", lambda e, cpar=cpar: e.tensor_copy(
                CTXS[:, cpar * 128:(cpar + 1) * 128], PS4[:, 0:128]),
                inc="dv")
            p1.wait("sy", rs["ev_ctxev_row"])
            ev_spill[row_] = p1.dma(
                "sy", ctx_dram[hh, qt, :, :],
                CTXS[:, cpar * 128:(cpar + 1) * 128], "qS")

        for g in range(4):
            if g >= 1:
                p1.wait("gp", ev_projr)
                ev_w = None
                for wbuf, wsrc in ((WQB, wq_r), (WKB, wk_r), (WVB, wv_r)):
                    ev_w = p1.dma("gp", wbuf[:], wsrc[:, :, g * 512:(g + 1) * 512], "qW")
                ev_wcur = ev_w
            else:
                ev_wcur = ev_wg0

            emit_tail(p1)
            emit_tail(p1)

            unit = 0
            last_bank_ev = {0: None, 1: None}

            def bank(u):
                return (PS4, 0) if u % 2 == 0 else (PS5, 1)

            if rs["ev_ctxev_row"] is not None:
                p1.wait("pe", rs["ev_ctxev_row"])
            p1.wait("pe", ev_wcur)
            if rs["ev_ctx_row"] is not None:
                p1.wait("ac", rs["ev_ctx_row"])
                p1.wait("dv", rs["ev_ctx_row"])
            for kt in range(NSLOT):
                pb, bi = bank(unit)
                if last_bank_ev[bi] is not None:
                    p1.wait("pe", last_bank_ev[bi])
                ev_mm = None
                for d_ in range(16):
                    ev_mm = p1.op("pe", lambda e, pb=pb, d=d_, kt=kt: e.matmul(
                        pb[:, 0:512],
                        H1[:, d * 2048 + kt * 128: d * 2048 + kt * 128 + 128],
                        WVB[:, d, :], start=(d == 0), stop=(d == 15)),
                        inc="pe" if d_ == 15 else None)
                p1.wait("ac", ev_mm)
                evv = p1.op("ac", lambda e, pb=pb, kt=kt: e.copy(
                    vsl(kt, 0, 512), pb[:, 0:512]), inc="ac")
                last_bank_ev[bi] = evv
                unit += 1
            ev_t3 = None
            for typ in ("k", "q"):
                wb = WKB if typ == "k" else WQB
                ncc = 4 if typ == "k" else 2
                base = 0 if typ == "k" else 1024
                for h in range(4):
                    for cc in range(ncc):
                        pb, bi = bank(unit)
                        if last_bank_ev[bi] is not None:
                            p1.wait("pe", last_bank_ev[bi])
                        ev_mm = None
                        for d_ in range(16):
                            ev_mm = p1.op("pe",
                                lambda e, pb=pb, d=d_, h=h, cc=cc, wb=wb, base=base: e.matmul(
                                    pb[:, 0:512],
                                    wb[:, d, h * 128:(h + 1) * 128],
                                    H1[:, d * 2048 + base + cc * 512:
                                       d * 2048 + base + (cc + 1) * 512],
                                    start=(d == 0), stop=(d == 15)),
                                inc="pe" if d_ == 15 else None)
                        p1.wait("ac", ev_mm)
                        if ev_t3 is not None:
                            p1.wait("ac", ev_t3)
                        ev_ev = p1.op("ac", lambda e, pb=pb: e.copy(
                            TMS[:], pb[:, 0:512]), inc="ac")
                        p1.wait("dv", ev_ev)
                        ev_t3 = p1.opw("dv", lambda e, cc=cc, base=base: e.tensor_mul(
                            TMS3[:], TMS[:],
                            COSF[:, base + cc * 512: base + (cc + 1) * 512]))
                        p1.wait("pe", ev_ev)
                        ev_sw = p1.op("pe", lambda e, pb=pb: e.matmul(
                            pb[:, 0:512], PSW[:], TMS[:], start=True, stop=True),
                            inc="pe")
                        p1.wait("dv", ev_sw)
                        p1.opw("dv", lambda e, pb=pb, cc=cc, base=base: e.tensor_mul(
                            TMS4[:], pb[:, 0:512],
                            SINF[:, base + cc * 512: base + (cc + 1) * 512]))
                        if typ == "k":
                            dstap = kfm(h, cc * 512, 512)
                        else:
                            dstap = QF[:, h * 1024 + cc * 512: h * 1024 + (cc + 1) * 512]
                        evr = p1.op("dv", lambda e, o=dstap: e.tensor_add(
                            o, TMS3[:], TMS4[:]), inc="dv")
                        last_bank_ev[bi] = evr
                        unit += 1
            ev_projr = ev_mm

            ev_bias = None
            for h_ in range(4):
                if h_ == 0:
                    p1.wait("dv", rs["ev_ctx_row"])
                ev_q2 = p1.op("dv", lambda e, h=h_: e.tensor_mul(
                    AT2[:, 0:1024], QF[:, h * 1024:(h + 1) * 1024],
                    QF[:, h * 1024:(h + 1) * 1024]), inc="dv")
                p1.wait("pe", ev_q2)
                if h_ == 0:
                    p1.wait("pe", rs["ev_exp_row"])
                ev_qm = None
                for c_ in range(2):
                    ev_qm = p1.op("pe", lambda e, c=c_: e.matmul(
                        PSBIG[0:1, c * 512:(c + 1) * 512], ONESB[:],
                        AT2[:, c * 512:(c + 1) * 512],
                        start=True, stop=True), inc="pe" if c_ == 1 else None)
                p1.wait("dv", ev_qm)
                p1.opw("dv", lambda e: e.reduce_max(
                    STAT[0:1, 92:93], PSBIG[0:1, 0:1024], axis=AX))
                ev_k2 = p1.op("dv", lambda e, h=h_: e.tensor_mul(
                    AT2[:, 0:2048], kfm(h, 0, 2048), kfm(h, 0, 2048)),
                    inc="dv")
                p1.wait("pe", ev_k2)
                ev_km = None
                for c_ in range(4):
                    ev_km = p1.op("pe", lambda e, c=c_: e.matmul(
                        PSBIG[0:1, c * 512:(c + 1) * 512], ONESB[:],
                        AT2[:, c * 512:(c + 1) * 512],
                        start=True, stop=True), inc="pe" if c_ == 3 else None)
                p1.wait("dv", ev_km)
                p1.opw("dv", lambda e: e.reduce_max(
                    STAT[0:1, 93:94], PSBIG[0:1, 0:2048], axis=AX))
                ev_prod = p1.opw("dv", lambda e: e.tensor_mul(
                    STAT[0:1, 92:93], STAT[0:1, 92:93], STAT[0:1, 93:94]))
                p1.wait("ac", ev_prod)
                ev_sqr = p1.opw("ac", lambda e: e.activation(
                    STAT[0:1, 92:93], STAT[0:1, 92:93], AF.Sqrt))
                p1.wait("dv", ev_sqr)
                ev_sc2 = p1.opw("dv", lambda e: e.tensor_scalar_mul(
                    STAT[0:1, 92:93], STAT[0:1, 92:93], -1.02 * INV_SQRT_HD))
                p1.wait("pe", ev_sc2)
                if h_ == 0:
                    p1.wait("pe", rs["ev_ctxev_row"])
                ev_bc = p1.op("pe", lambda e: e.matmul(
                    PS4[:, 0:1], ONESR[:], STAT[0:1, 92:93],
                    start=True, stop=True), inc="pe")
                p1.wait("ac", ev_bc)
                ev_bias = p1.op("ac", lambda e, h=h_: e.copy(
                    STAT[:, 88 + h:89 + h], PS4[:, 0:1]), inc="ac")
            rs["ev_ctxev_row"] = ev_bias

            if g == 3 and "d_kfm" in tap_t:
                p1.wait("sy", last_bank_ev[0])
                p1.wait("sy", last_bank_ev[1])
                p1.dma("sy", tap_t["d_kfm"], KV2[:, 0:8192], "qS")
                p1.dma("sy", tap_t["d_qfm"], QF[:], "qS")
                p1.dma("sy", tap_t["d_vsb"], KV2[:, 8192:16384], "qS")

            for h in range(4):
                hh = g * 4 + h
                for qt in range(NT):
                    col = hh * 8 + qt
                    kn = 9 + qt
                    kctx = kn * 128
                    par = row % 3
                    attn = ATT[:, par * 2048: par * 2048 + 2048]
                    p1.wait("pe", rs["ev_exp_row"])
                    nch = (kctx + 511) // 512
                    ev_sc = None
                    for c in range(nch):
                        w = min(512, kctx - c * 512)
                        ev_sc = p1.op("pe", lambda e, c=c, w=w, h=h, qt=qt: e.matmul(
                            PSBIG[:, c * 512: c * 512 + w],
                            QF[:, h * 1024 + qt * 128: h * 1024 + (qt + 1) * 128],
                            kfm(h, c * 512, w), start=True, stop=True),
                            inc="pe" if c == nch - 1 else None)
                    if len(rs["pq"]) >= 2:
                        emit_tail(p1)
                    p1.wait("ac", ev_sc)
                    if row >= 3:
                        p1.wait("ac", ev_trrow[row - 3])
                    p1.op("ac", lambda e, attn=attn, col=col, h=h: e.activation(
                        attn[:, 0:1024], PSBIG[:, 0:1024], AF.Exp,
                        bias=STAT[:, 88 + h:89 + h], scale=INV_SQRT_HD,
                        accum_out=GRID[:, col:col + 1]))
                    if kn > 9:
                        p1.op("ac", lambda e, attn=attn, col=col, kn=kn, h=h:
                            e.activation(
                                attn[:, 1024:(kn - 1) * 128],
                                PSBIG[:, 1024:(kn - 1) * 128], AF.Exp,
                                bias=STAT[:, 88 + h:89 + h], scale=INV_SQRT_HD,
                                accum_out=GRID[:, 128 + col:129 + col]))
                    ev_exp = p1.op("ac", lambda e, attn=attn, kn=kn, h=h:
                        e.activation(
                            attn[:, (kn - 1) * 128: kn * 128],
                            PSBIG[:, (kn - 1) * 128: kn * 128], AF.Exp,
                            bias=STAT[:, 88 + h:89 + h], scale=INV_SQRT_HD),
                        inc="ac")
                    rs["ev_exp_row"] = ev_exp
                    p1.wait("dv", ev_exp)
                    p1.opw("dv", lambda e, attn=attn, kn=kn: e.tensor_mul(
                        attn[:, (kn - 1) * 128: kn * 128],
                        attn[:, (kn - 1) * 128: kn * 128], TRI01[:]))
                    p1.opw("dv", lambda e, attn=attn, kn=kn: e.reduce_sum(
                        STAT[:, 84:85], attn[:, (kn - 1) * 128: kn * 128],
                        axis=AX))
                    if kn > 9:
                        p1.opw("dv", lambda e, col=col: e.scalar_tensor_tensor(
                            out=STAT[:, 82:83], in0=GRID[:, col:col + 1],
                            scalar=FLG[:, 0:1], in1=GRID[:, 128 + col:129 + col],
                            op0=ALU.mult, op1=ALU.add))
                    else:
                        p1.opw("dv", lambda e, col=col: e.tensor_scalar(
                            out=STAT[:, 82:83], in0=GRID[:, col:col + 1],
                            scalar1=FLG[:, 0:1], scalar2=None, op0=ALU.mult))
                    p1.opw("dv", lambda e: e.tensor_add(
                        STAT[:, 82:83], STAT[:, 82:83], STAT[:, 84:85]))
                    p1.opw("dv", lambda e: e.reciprocal(STAT[:, 83:84], STAT[:, 82:83]))
                    ev_nrm = p1.op("dv", lambda e, attn=attn, kctx=kctx:
                        e.tensor_scalar_mul(
                            attn[:, 0:kctx], attn[:, 0:kctx], STAT[:, 83:84]),
                        inc="dv")
                    rs["pq"].append({"h": h, "hh": hh, "qt": qt, "row": row,
                                     "kn": kn, "attn": attn, "ev_nrm": ev_nrm})
                    row += 1
        emit_tail(p1)
        emit_tail(p1)
        p1.emit(nc)
        ev_spill_all = ("qS", cnt["qS"])

    esA.close()

    ev_x2st = [None] * NT
    if stage >= 3:
        esB = ExitStack()

        def sbB(name, shape, dt):
            return esB.enter_context(nc.sbuf_tensor(name, shape, dt))

        WOB = sbB("WOB", [128, 16, 2048], BF16)
        CTXB = sbB("CTXB", [128, 2, 16, 128], BF16)
        XT = sbB("XT", [128, 2, 2048], F32)
        XNEW = sbB("XNEW", [128, 2048], F32)
        XFM = sbB("XFM", [128, 2048], F32)
        HT2 = sbB("HT2", [128, 2048], BF16)
        SQ2 = sbB("SQ2", [128, 2048], BF16)
        LG = sbB("LG", [128, 64], F32)

        p2 = Plan(sems, cnt)
        ev_wo = None
        for hh in range(16):
            ev_wo = p2.dma("gp", WOB[:, hh, :], wo_r[:, hh, :], "qW")
        ev_sp = None
        for d_ in range(16):
            ev_sp = p2.op("pe", lambda e, d=d_: e.matmul(
                PS5[0:1, 0:8], ONESF[:], GWTB[:, d, :],
                start=(d == 0), stop=(d == 15)), inc="pe" if d_ == 15 else None)
        p2.wait("dv", ev_sp)
        ev_srv = p2.opw("dv", lambda e: e.tensor_copy(SRV[:], PS5[0:1, 0:8]))
        p2.wait("pe", ev_srv)
        ev_sbp = p2.op("pe", lambda e: e.matmul(
            PS5[:, 16:24], ONESR[:], SRV[:], start=True, stop=True), inc="pe")
        p2.wait("dv", ev_sbp)
        ev_sbc = p2.opw("dv", lambda e: e.tensor_copy(SBC[:], PS5[:, 16:24]))

        ev_ctxb = [None, None]
        ev_xt = [None, None]
        ev_wolast = [None, None]
        ev_xtfree = [None, None]
        ev_ep = {0: ev_sbc, 1: ev_sbc}
        ev_gtr = None
        ev_s2prev = None
        ev_xfm_ev = None
        ev_h2w = None
        ev_wm = None
        for qt in range(NT):
            par = qt % 2
            if qt >= 2:
                p2.wait("sy", ev_wolast[par])
                p2.wait("sy", ev_xtfree[par])
            if qt == 0:
                p2.wait("sy", ev_spill_all)
            ev_ctxb[par] = p2.dma("sy", CTXB[:, par, :, :], ctx_rd[:, qt, :, :], "qX")
            ev_xt[par] = p2.dma("sy", XT[:, par, :], x_r[:, 8 + qt, :], "qX")
            for dc in range(4):
                pb = PS4 if dc % 2 == 0 else PS5
                p2.wait("pe", ev_ctxb[par])
                if qt == 0 and dc == 0:
                    p2.wait("pe", ev_wo)
                p2.wait("pe", ev_ep[dc % 2])
                ev_mm = None
                for hh in range(16):
                    ev_mm = p2.op("pe",
                        lambda e, pb=pb, hh=hh, dc=dc, par=par: e.matmul(
                            pb[:, 0:512], CTXB[:, par, hh, :],
                            WOB[:, hh, dc * 512:(dc + 1) * 512],
                            start=(hh == 0), stop=(hh == 15)),
                        inc="pe" if hh == 15 else None)
                if dc == 3:
                    ev_wolast[par] = ev_mm
                p2.wait("dv", ev_mm)
                p2.wait("dv", ev_xt[par])
                if dc == 0 and qt > 0:
                    p2.wait("dv", ev_gtr)
                    p2.wait("dv", ev_s2prev)
                    p2.wait("dv", ev_x2st[qt - 1])
                ev_ep[dc % 2] = p2.op("dv",
                    lambda e, pb=pb, dc=dc, par=par: e.tensor_add(
                        XNEW[:, dc * 512:(dc + 1) * 512], pb[:, 0:512],
                        XT[:, par, dc * 512:(dc + 1) * 512]), inc="dv")
                if dc == 3:
                    ev_xtfree[par] = ev_ep[1]
            ev_xn = ev_ep[1]
            p2.wait("sy", ev_xn)
            ev_x2st[qt] = p2.dma("sy", out_r[:, qt, :], XNEW[:], "qS")
            p2.wait("dv", ev_xn)
            p2.opw("dv", lambda e, qt=qt: e.reduce_sum(
                STAT[:, qt:qt + 1], XNEW[:], axis=AX))
            p2.wait("ac", ev_xn)
            ev_s2 = p2.op("ac", lambda e, qt=qt: e.activation(
                SQ2[:], XNEW[:], AF.Square, bias=0.0, scale=1.0,
                accum_out=STAT[:, 16 + qt:17 + qt]), inc="ac")
            p2.wait("ac", ev_s2)
            ev_s2prev = ev_s2
            p2.wait("dv", ev_s2)
            p2.opw("dv", lambda e, qt=qt: e.tensor_scalar_mul(
                STAT[:, 32 + qt:33 + qt], STAT[:, qt:qt + 1], -1.0 / D))
            ev_pre = p2.opw("dv", lambda e, qt=qt: e.tensor_scalar_mul(
                STAT[:, 48 + qt:49 + qt], STAT[:, 16 + qt:17 + qt], 1.0 / D))
            p2.wait("ac", ev_pre)
            ev_mu2 = p2.opw("ac", lambda e, qt=qt: e.activation(
                STAT[:, 64 + qt:65 + qt], STAT[:, 32 + qt:33 + qt], AF.Square))
            p2.wait("dv", ev_mu2)
            ev_var = p2.opw("dv", lambda e, qt=qt: e.tensor_sub(
                STAT[:, 64 + qt:65 + qt], STAT[:, 48 + qt:49 + qt],
                STAT[:, 64 + qt:65 + qt]))
            p2.wait("ac", ev_var)
            p2.opw("ac", lambda e, qt=qt: e.activation(
                STAT[:, 64 + qt:65 + qt], STAT[:, 64 + qt:65 + qt], AF.Ln, bias=EPS))
            ev_rstd = p2.opw("ac", lambda e, qt=qt: e.activation(
                STAT[:, 64 + qt:65 + qt], STAT[:, 64 + qt:65 + qt], AF.Exp,
                scale=-0.5))
            p2.wait("dv", ev_rstd)
            if ev_h2w is not None:
                p2.wait("dv", ev_h2w)
            ev_n2 = p2.op("dv", lambda e, qt=qt: e.tensor_scalar(
                out=HT2[:], in0=XNEW[:], scalar1=STAT[:, 32 + qt:33 + qt],
                scalar2=STAT[:, 64 + qt:65 + qt], op0=ALU.add, op1=ALU.mult),
                inc="dv")
            for half in range(2):
                dst = bass.AP(KV2, (half * 8) * 1024 + qt * 128,
                              [[16384, 128], [1024, 8], [1, 128]])
                ins = [HT2[:, (half * 8 + dc) * 128:(half * 8 + dc + 1) * 128]
                       for dc in range(8)]
                ev_h2w = tr_batch(p2, ins, dst, eng="dv",
                                  pre_pe_ev=(ev_n2 if half == 0 else None))
            if ev_xfm_ev is not None:
                p2.wait("pe", ev_xfm_ev)
            ev_gt = None
            for dd in range(16):
                ev_gt = p2.op("pe", lambda e, dd=dd: e.transpose(
                    PSBIG[:, dd * 128:(dd + 1) * 128],
                    XNEW[:, dd * 128:(dd + 1) * 128], IDTF[:]),
                    inc="pe" if dd == 15 else None)
            ev_gtr = ev_gt
            p2.wait("dv", ev_gt)
            ev_xfm_ev = p2.op("dv", lambda e: e.tensor_copy(XFM[:], PSBIG[:]),
                              inc="dv")
            p2.wait("pe", ev_ep[0])
            p2.wait("pe", ev_xfm_ev)
            ev_lgm = None
            for d_ in range(16):
                ev_lgm = p2.op("pe", lambda e, d=d_: e.matmul(
                    PS4[:, 0:8], XFM[:, d * 128:(d + 1) * 128], GWTB[:, d, :],
                    start=(d == 0), stop=(d == 15)), inc="pe" if d_ == 15 else None)
            p2.wait("dv", ev_lgm)
            p2.opw("dv", lambda e, qt=qt: e.scalar_tensor_tensor(
                out=LG[:, 0:8], in0=SBC[:], scalar=STAT[:, 32 + qt:33 + qt],
                in1=PS4[:, 0:8], op0=ALU.mult, op1=ALU.add))
            ev_lg = p2.opw("dv", lambda e, qt=qt: e.tensor_scalar_mul(
                LG[:, 0:8], LG[:, 0:8], STAT[:, 64 + qt:65 + qt]))
            ev_ep[0] = ev_lg
            ev_rm = p2.opw("dv", lambda e: e.reduce_max(
                STAT[:, 84:85], LG[:, 0:8], axis=AX, negate=True))
            p2.wait("ac", ev_rm)
            ev_sm = p2.op("ac", lambda e: e.activation(
                LG[:, 8:16], LG[:, 0:8], AF.Exp, bias=STAT[:, 84:85],
                accum_out=STAT[:, 85:86]), inc="ac")
            p2.wait("ac", ev_sm)
            p2.wait("dv", ev_sm)
            p2.opw("dv", lambda e: e.reciprocal(STAT[:, 86:87], STAT[:, 85:86]))
            p2.opw("dv", lambda e: e.tensor_scalar_mul(
                LG[:, 8:16], LG[:, 8:16], STAT[:, 86:87]))
            sc = LG[:, 8:16].rearrange("p (a s) -> p a s", s=2)
            p2.opw("dv", lambda e, sc=sc: e.tensor_max(
                LG[:, 16:20], sc[:, :, 0], sc[:, :, 1]))
            p2.opw("dv", lambda e: e.reduce_max(STAT[:, 87:88], LG[:, 16:20], axis=AX))
            p2.opw("dv", lambda e: e.tensor_scalar(
                out=LG[:, 20:24], in0=LG[:, 16:20], scalar1=STAT[:, 87:88],
                scalar2=None, op0=ALU.is_equal))
            p2.opw("dv", lambda e: e.scalar_tensor_tensor(
                out=LG[:, 24:28], in0=LG[:, 20:24], scalar=-1.0e30,
                in1=LG[:, 16:20], op0=ALU.mult, op1=ALU.add))
            p2.opw("dv", lambda e: e.reduce_max(STAT[:, 88:89], LG[:, 24:28], axis=AX))
            p2.opw("dv", lambda e: e.tensor_scalar(
                out=LG[:, 20:24], in0=LG[:, 16:20], scalar1=STAT[:, 88:89],
                scalar2=None, op0=ALU.is_ge))
            mk = LG[:, 32:40].rearrange("p (a s) -> p a s", s=2)
            p2.opw("dv", lambda e, sc=sc, mk=mk: e.tensor_mul(
                mk[:, :, 0], sc[:, :, 0], LG[:, 20:24]))
            p2.opw("dv", lambda e, sc=sc, mk=mk: e.tensor_mul(
                mk[:, :, 1], sc[:, :, 1], LG[:, 20:24]))
            p2.opw("dv", lambda e: e.reduce_max(STAT[:, 87:88], LG[:, 32:40], axis=AX))
            p2.opw("dv", lambda e: e.tensor_scalar(
                out=LG[:, 40:48], in0=LG[:, 32:40], scalar1=STAT[:, 87:88],
                scalar2=None, op0=ALU.is_equal))
            p2.opw("dv", lambda e: e.scalar_tensor_tensor(
                out=LG[:, 48:56], in0=LG[:, 40:48], scalar=-1.0e30,
                in1=LG[:, 32:40], op0=ALU.mult, op1=ALU.add))
            p2.opw("dv", lambda e: e.reduce_max(STAT[:, 88:89], LG[:, 48:56], axis=AX))
            p2.opw("dv", lambda e: e.tensor_scalar(
                out=LG[:, 40:48], in0=LG[:, 32:40], scalar1=STAT[:, 88:89],
                scalar2=None, op0=ALU.is_ge))
            ev_wm = p2.opw("dv", lambda e, qt=qt: e.tensor_mul(
                WMAT[:, qt * 8:(qt + 1) * 8], LG[:, 32:40], LG[:, 40:48]))
        if "d_h2" in tap_t:
            p2.wait("sy", ev_h2w)
            p2.dma("sy", tap_t["d_h2"], KV2[:], "qS")
        if "d_wmat" in tap_t:
            p2.wait("sy", ev_wm)
            p2.dma("sy", tap_t["d_wmat"], WMAT[:], "qS")
        if "d_logit" in tap_t:
            p2.wait("sy", ev_wm)
            p2.dma("sy", tap_t["d_logit"], WMAT[:], "qS")
        p2.emit(nc)
        esB.close()

    if stage >= 4:
        esC = ExitStack()

        def sbC(name, shape, dt):
            return esC.enter_context(nc.sbuf_tensor(name, shape, dt))

        GUB = sbC("GUB", [128, 3, 16, 384], BF16)
        UUB = sbC("UUB", [128, 3, 16, 384], BF16)
        DOWNB = sbC("DOWNB", [128, 11, 2048], BF16)
        ACTA = sbC("ACTA", [128, NT, FE], BF16)
        ACTT = sbC("ACTT", [128, 2, FE], BF16)
        SGB = sbC("SGB", [128, 512], BF16)
        ESB3 = sbC("ESB3", [128, 2048], F32)

        p3 = Plan(sems, cnt)
        FCS = ((0, 384), (384, 384), (768, 384), (1152, 256))
        experts = [("sh", 0), ("sh", 1)] + [("e", n) for n in range(NE)]
        gfc = 0
        ev_gu_cons = {}
        ev_down_done = None
        ev_tr_exp = None
        ev_store = {}
        ev_dn_tt = {}
        ev_evac_h = [None, None]
        ev_silu = None
        ev_stt = None
        for mi, (kind, idx) in enumerate(experts):
            def gu_src(off, w, kind=kind, idx=idx):
                if kind == "sh":
                    return (shg_r[:, :, idx * FE + off: idx * FE + off + w],
                            shu_r[:, :, idx * FE + off: idx * FE + off + w])
                return (gate_t[idx].rearrange("(c p) f -> p c f", p=128)[
                            :, :, off:off + w],
                        up_t[idx].rearrange("(c p) f -> p c f", p=128)[
                            :, :, off:off + w])

            def emit_gu_load(gfc_, off, w):
                gpar = gfc_ % 3
                p3.wait("gp", ev_gu_cons.get(gfc_ - 3))
                gsrc, usrc = gu_src(off, w)
                p3.dma("gp", GUB[:, gpar, :, 0:w], gsrc, "qW")
                return p3.dma("gp", UUB[:, gpar, :, 0:w], usrc, "qW")

            myfc = []
            for fci, (off, w) in enumerate(FCS):
                gpar = gfc % 3
                evu = emit_gu_load(gfc, off, w) if fci < 3 else None
                myfc.append([gfc, gpar, off, w, evu])
                gfc += 1
            p3.wait("gp", ev_down_done)
            ev_db = None
            for fc in range(11):
                if kind == "sh":
                    dsrc = shd_t[idx * FE + fc * 128: idx * FE + (fc + 1) * 128, :]
                else:
                    dsrc = down_t[idx][fc * 128:(fc + 1) * 128, :]
                ev_db = p3.dma("gp", DOWNB[:, fc, :], dsrc, "qW")
            for fci, rec in enumerate(myfc):
                (gfci, gpar, off, w, evu) = rec
                for tt in range(NT):
                    p3.wait("pe", evu)
                    if ev_silu is not None:
                        p3.wait("pe", ev_silu)
                    ev_g = None
                    for d_ in range(16):
                        ev_g = p3.op("pe",
                            lambda e, d=d_, tt=tt, gpar=gpar, w=w: e.matmul(
                                PS4[:, 0:w], h2sl(d, tt * 128, 128),
                                GUB[:, gpar, d, 0:w],
                                start=(d == 0), stop=(d == 15)),
                            inc="pe" if d_ == 15 else None)
                    p3.wait("ac", ev_g)
                    if ev_stt is not None:
                        p3.wait("ac", ev_stt)
                    ev_silu = p3.op("ac", lambda e, w=w: e.activation(
                        SGB[:, 0:w], PS4[:, 0:w], AF.Silu), inc="ac")
                    if ev_stt is not None:
                        p3.wait("pe", ev_stt)
                    ev_u = None
                    for d_ in range(16):
                        ev_u = p3.op("pe",
                            lambda e, d=d_, tt=tt, gpar=gpar, w=w: e.matmul(
                                PS5[:, 0:w], h2sl(d, tt * 128, 128),
                                UUB[:, gpar, d, 0:w],
                                start=(d == 0), stop=(d == 15)),
                            inc="pe" if d_ == 15 else None)
                    p3.wait("dv", ev_u)
                    p3.wait("dv", ev_silu)
                    if ev_tr_exp is not None:
                        p3.wait("dv", ev_tr_exp)
                    if kind == "sh":
                        ev_stt = p3.op("dv",
                            lambda e, tt=tt, off=off, w=w: e.scalar_tensor_tensor(
                                out=ACTA[:, tt, off:off + w], in0=PS5[:, 0:w],
                                scalar=1.0, in1=SGB[:, 0:w],
                                op0=ALU.mult, op1=ALU.mult), inc="dv")
                    else:
                        ev_stt = p3.op("dv",
                            lambda e, tt=tt, off=off, w=w, n=idx:
                            e.scalar_tensor_tensor(
                                out=ACTA[:, tt, off:off + w], in0=PS5[:, 0:w],
                                scalar=WMAT[:, tt * 8 + n: tt * 8 + n + 1],
                                in1=SGB[:, 0:w], op0=ALU.mult, op1=ALU.mult),
                            inc="dv")
                ev_gu_cons[gfci] = ev_stt
                if fci == 0:
                    off2, w2 = FCS[3]
                    myfc[3][4] = emit_gu_load(myfc[3][0], off2, w2)
            for tt in range(NT):
                gtt = mi * NT + tt
                apar = gtt % 2
                p3.wait("dv", ev_dn_tt.get(gtt - 2))
                p3.wait("ac", ev_dn_tt.get(gtt - 2))
                ev_t1 = tr_batch(p3, [ACTA[:, tt, c * 128:(c + 1) * 128]
                                      for c in range(8)],
                                 ACTT[:, apar, 0:1024], eng="dv",
                                 pre_pe_ev=(ev_stt if tt == 0 else None))
                ev_t2 = tr_batch(p3, [ACTA[:, tt, c * 128:(c + 1) * 128]
                                      for c in range(8, 11)],
                                 ACTT[:, apar, 1024:1408], eng="ac")
                if tt == NT - 1:
                    ev_tr_exp = ("pe", cnt["pe"])
                p3.wait("pe", ev_db)
                p3.wait("pe", ev_t1)
                p3.wait("pe", ev_t2)
                for half in range(2):
                    p3.wait("pe", ev_evac_h[half])
                    ev_dn = None
                    for fc in range(11):
                        for dc in range(2):
                            co = half * 1024 + dc * 512
                            ev_dn = p3.op("pe", lambda e, fc=fc, co=co, apar=apar:
                                e.matmul(
                                    PSBIG[:, co: co + 512],
                                    ACTT[:, apar, fc * 128:(fc + 1) * 128],
                                    DOWNB[:, fc, co: co + 512],
                                    start=(fc == 0), stop=(fc == 10)),
                                inc="pe" if (fc == 10 and dc == 1) else None)
                    eng = "dv" if half == 0 else "ac"
                    p3.wait(eng, ev_dn)
                    p3.wait(eng, ev_store.get((tt, half)))
                    hsl = slice(half * 1024, (half + 1) * 1024)
                    if eng == "dv":
                        ev_evac_h[half] = p3.op(eng, lambda e, hsl=hsl:
                            e.tensor_copy(ESB3[:, hsl], PSBIG[:, hsl]), inc=eng)
                    else:
                        ev_evac_h[half] = p3.op(eng, lambda e, hsl=hsl:
                            e.copy(ESB3[:, hsl], PSBIG[:, hsl]), inc=eng)
                    p3.wait("gp", ev_evac_h[half])
                    p3.wait("gp", ev_store.get((tt, half)))
                    if mi == 0:
                        p3.wait("gp", ev_x2st[tt])
                    ev_store[(tt, half)] = p3.dma(
                        "gp", out_r[:, tt, hsl], ESB3[:, hsl], "qA", accum=True)
                ev_dn_tt[gtt] = ev_dn
                ev_down_done = ev_dn
        if "d_acta" in tap_t:
            p3.wait("sy", ev_stt)
            p3.dma("sy", tap_t["d_acta"], ACTA[:], "qS")
        p3.op("gp", lambda e: e.engine_nop())
        p3.wait("gp", ("qA", cnt["qA"]))
        p3.emit(nc)
        esC.close()

    esP.close()
    return nc, tap_t


def _rope_tables(j):
    if j == 1:
        pos = np.arange(S, dtype=np.float32)
    else:
        pos = np.concatenate([np.zeros(1024, np.float32),
                              np.arange(1024, dtype=np.float32)])
    freqs = 1.0 / 10000.0 ** (np.arange(0, HD, 2, dtype=np.float32) / HD)
    ang = np.outer(pos, freqs)
    c = np.cos(ang).T.astype(np.float32)
    s_ = np.sin(ang).T.astype(np.float32)
    cosf = np.concatenate([c, c], axis=0)
    sinf = np.concatenate([-s_, s_], axis=0)
    return np.ascontiguousarray(cosf), np.ascontiguousarray(sinf)


def stage_inputs(inputs):
    x = np.asarray(inputs["L0_attn_x"], np.float32)
    perm = np.concatenate([np.arange(0, HD, 2), np.arange(1, HD, 2)])
    permf = np.concatenate([h * HD + perm for h in range(H)])
    wq_p = np.ascontiguousarray(np.asarray(inputs["L0_attn_Wq"], np.float32)[:, permf])
    wk_p = np.ascontiguousarray(np.asarray(inputs["L0_attn_Wk"], np.float32)[:, permf])
    wv = np.ascontiguousarray(np.asarray(inputs["L0_attn_Wv"], np.float32))
    wo = np.ascontiguousarray(np.asarray(inputs["L0_attn_Wo"], np.float32))
    gwt = np.ascontiguousarray(np.asarray(inputs["L0_ffn_gate_weight"], np.float32).T)
    gate_t = np.ascontiguousarray(
        np.asarray(inputs["L0_ffn_gate_w"], np.float32).transpose(0, 2, 1))
    up_t = np.ascontiguousarray(
        np.asarray(inputs["L0_ffn_up_w"], np.float32).transpose(0, 2, 1))
    down_t = np.ascontiguousarray(
        np.asarray(inputs["L0_ffn_down_w"], np.float32).transpose(0, 2, 1))
    shg_t = np.ascontiguousarray(np.asarray(inputs["L0_ffn_sh_gate_w"], np.float32).T)
    shu_t = np.ascontiguousarray(np.asarray(inputs["L0_ffn_sh_up_w"], np.float32).T)
    shd_t = np.ascontiguousarray(np.asarray(inputs["L0_ffn_sh_down_w"], np.float32).T)

    idt = np.eye(128, dtype=np.float32).astype(ml_dtypes.bfloat16)
    idtf = np.eye(128, dtype=np.float32)
    pswap = np.zeros((128, 128), np.float32)
    for i in range(64):
        pswap[i, i + 64] = 1.0
        pswap[i + 64, i] = 1.0
    pswap = pswap.astype(ml_dtypes.bfloat16)
    tri = np.where(np.arange(128)[None, :] > np.arange(128)[:, None],
                   np.float32(-1.0e30), np.float32(0.0)).astype(np.float32)
    tri01 = np.where(np.arange(128)[None, :] > np.arange(128)[:, None],
                     np.float32(0.0), np.float32(1.0)).astype(ml_dtypes.bfloat16)
    onesb = np.ones((128, 1), ml_dtypes.bfloat16)
    onesf = np.ones((128, 1), np.float32)
    onesr = np.ones((1, 128), np.float32)

    in_maps = []
    for c in range(8):
        b, j = divmod(c, 2)
        xs = np.zeros((S, D), np.float32)
        if j == 1:
            xs[:] = x[b]
        else:
            xs[1024:] = x[b, 0:1024]
        cosf, sinf = _rope_tables(j)
        flg = np.full((128, 1), float(j), np.float32)
        in_maps.append({
            "x": xs, "wq": wq_p, "wk": wk_p, "wv": wv, "wo": wo, "gwt": gwt,
            "gate_t": gate_t, "up_t": up_t, "down_t": down_t,
            "shg_t": shg_t, "shu_t": shu_t, "shd_t": shd_t,
            "cosf": cosf, "sinf": sinf, "idt": idt, "idtf": idtf,
            "psw": pswap, "tri": tri, "tri01": tri01, "onesb": onesb,
            "flg": flg,
            "onesf": onesf, "onesr": onesr,
            "cvec": np.broadcast_to(
                np.asarray(CONST_VALS, np.float32)[None, :],
                (128, len(CONST_VALS))).copy(),
        })
    return in_maps


def kernel(**inputs):
    in_maps = stage_inputs(inputs)
    nc, _ = build(stage=4)
    res = run_bass_kernel_spmd(nc, in_maps, core_ids=list(range(8)))
    outf = np.empty((B, S, D), np.float32)
    for c in range(8):
        b, j = divmod(c, 2)
        outf[b, j * 1024:(j + 1) * 1024] = res.results[c]["out"]
    return outf


if __name__ == "__main__":
    import sys
    st = int(sys.argv[1]) if len(sys.argv) > 1 else 4
    nc, _ = build(stage=st)
    print("built stage", st, "instructions:", nc._state.n_instructions())


# version 17
# speedup vs baseline: 1.1824x; 1.0761x over previous
import numpy as np
import ml_dtypes
from contextlib import ExitStack

import concourse.bass as bass
import concourse.mybir as mybir
from concourse.bass_utils import run_bass_kernel_spmd

F32 = mybir.dt.float32
BF16 = mybir.dt.bfloat16
AF = mybir.ActivationFunctionType
ALU = mybir.AluOpType
AX = mybir.AxisListType.X

B, S, D = 4, 2048, 2048
H, HD = 16, 128
NSLOT = 16
NT = 8
FE = 1408
FS = 2816
NE = 8
INV_SQRT_HD = 1.0 / float(np.sqrt(HD))
EPS = 1e-5
CONST_VALS = [EPS, -0.5, INV_SQRT_HD, -INV_SQRT_HD, 1.0 / D, -1.0 / D,
              -1.0e30, 1.0, 0.0, 2.0]


class Plan:
    ENGINES = ("pe", "dv", "ac", "gp", "sy")

    def __init__(self, sems, cnt):
        self.q = {e: [] for e in self.ENGINES}
        self.sems = sems
        self.cnt = cnt

    def op(self, eng, fn, inc=None, incn=1):
        ev = None
        if inc is not None:
            self.cnt[inc] += incn
            ev = (inc, self.cnt[inc])
        self.q[eng].append(("op", fn, inc, incn))
        return ev

    def wait(self, eng, ev):
        if ev is None:
            return
        self.q[eng].append(("wait", ev[0], ev[1]))

    def opw(self, eng, fn):
        ev = self.op(eng, fn, inc=eng)
        self.wait(eng, ev)
        return ev

    def dma(self, eng, out_ap, in_ap, inc, accum=False):
        def fn(e, o=out_ap, i=in_ap, a=accum):
            if a:
                return e.dma_start(o, i, accum_op=ALU.add)
            return e.dma_start(o, i)
        return self.op(eng, fn, inc=inc, incn=16)

    def replay(self, eng_name, e):
        for item in self.q[eng_name]:
            if item[0] == "op":
                _, fn, inc, incn = item
                inst = fn(e)
                if inc is not None:
                    inst.then_inc(self.sems[inc], incn)
            else:
                _, sem, val = item
                e.wait_ge(self.sems[sem], val)

    def emit(self, nc):
        with nc.Block() as blk:
            if self.q["gp"]:
                blk.gpsimd(lambda e: self.replay("gp", e))
            if self.q["sy"]:
                blk.sync(lambda e: self.replay("sy", e))
            if self.q["pe"]:
                blk.tensor(lambda e: self.replay("pe", e))
            if self.q["dv"]:
                blk.vector(lambda e: self.replay("dv", e))
            if self.q["ac"]:
                blk.scalar(lambda e: self.replay("ac", e))


def build(stage=4, taps=()):
    nc = bass.Bass("TRN2", target_bir_lowering=False)

    def din(name, shape, dt=F32):
        return nc.dram_tensor(name, shape, dt, kind="ExternalInput").ap()

    x_in = din("x", [S, D])
    wq = din("wq", [D, D])
    wk = din("wk", [D, D])
    wv = din("wv", [D, D])
    wo = din("wo", [D, D])
    gwt = din("gwt", [D, NE])
    gate_t = din("gate_t", [NE, D, FE])
    up_t = din("up_t", [NE, D, FE])
    down_t = din("down_t", [NE, FE, D])
    shg_t = din("shg_t", [D, FS])
    shu_t = din("shu_t", [D, FS])
    shd_t = din("shd_t", [FS, D])
    cosf_in = din("cosf", [128, S])
    sinf_in = din("sinf", [128, S])
    idt_in = din("idt", [128, 128], BF16)
    idtf_in = din("idtf", [128, 128])
    psw_in = din("psw", [128, 128], BF16)
    tri_in = din("tri", [128, 128])
    tri01_in = din("tri01", [128, 128], BF16)
    onesb_in = din("onesb", [128, 1], BF16)
    flg_in = din("flg", [128, 1])
    onesf_in = din("onesf", [128, 1])
    onesr_in = din("onesr", [1, 128])
    cvec_in = din("cvec", [128, len(CONST_VALS)])

    out = nc.dram_tensor("out", [1024, D], F32, kind="ExternalOutput").ap()
    tk = lambda n: "ExternalOutput" if n in taps else "Internal"
    ctx_dram = nc.dram_tensor("d_ctx", [H, NT, 128, 128], BF16, kind=tk("d_ctx")).ap()
    tap_t = {}
    for nm, shp, dt in (("d_h1", [128, NSLOT * 2048], BF16),
                        ("d_kfm", [128, 4 * 2048], BF16),
                        ("d_qfm", [128, 4 * 1024], BF16),
                        ("d_vsb", [128, NSLOT * 512], BF16),
                        ("d_h2", [128, NSLOT * 1024], BF16),
                        ("d_wmat", [128, NT * NE], F32),
                        ("d_stat", [128, 96], F32),
                        ("d_acta", [128, NT * FE], BF16),
                        ("d_logit", [128, NT * NE], F32)):
        if nm in taps:
            tap_t[nm] = nc.dram_tensor(nm, shp, dt, kind="ExternalOutput").ap()

    esP = ExitStack()

    def sbP(name, shape, dt):
        return esP.enter_context(nc.sbuf_tensor(name, shape, dt))

    def psP(name, shape, dt=F32):
        return esP.enter_context(nc.psum_tensor(name, shape, dt))

    IDT = sbP("IDT", [128, 128], BF16)
    IDTF = sbP("IDTF", [128, 128], F32)
    PSW = sbP("PSW", [128, 128], BF16)
    TRI = sbP("TRI", [128, 128], F32)
    TRI01 = sbP("TRI01", [128, 128], BF16)
    ONESB = sbP("ONESB", [128, 1], BF16)
    FLG = sbP("FLG", [128, 1], F32)
    ONESF = sbP("ONESF", [128, 1], F32)
    ONESR = sbP("ONESR", [1, 128], F32)
    GWTB = sbP("GWTB", [128, 16, NE], F32)
    KV2 = sbP("KV2", [128, 16384], BF16)
    GRID = sbP("GRID", [128, 256], F32)
    STAT = sbP("STAT", [128, 96], F32)
    WMAT = sbP("WMAT", [128, NT * NE], F32)
    SBC = sbP("SBC", [128, NE], F32)
    SRV = sbP("SRV", [1, NE], F32)
    CVEC = sbP("CVEC", [128, len(CONST_VALS)], F32)
    for ci, cv in enumerate(CONST_VALS):
        nc.const_aps.register(F32, cv, CVEC[:, ci:ci + 1])

    PSBIG = psP("PSBIG", [128, 2048])
    PS4 = psP("PS4", [128, 512])
    PS5 = psP("PS5", [128, 512])
    PSB6 = psP("PSB6", [128, 1024], BF16)
    PSB7 = psP("PSB7", [128, 1024], BF16)

    sems = {}
    cnt = {}
    for s_ in ("qW", "qX", "qS", "qA", "pe", "dv", "ac"):
        sems[s_] = esP.enter_context(nc.semaphore(s_))
        cnt[s_] = 0

    x_r = x_in.rearrange("(t p) d -> p t d", p=128)
    wq_r = wq.rearrange("(c p) n -> p c n", p=128)
    wk_r = wk.rearrange("(c p) n -> p c n", p=128)
    wv_r = wv.rearrange("(c p) n -> p c n", p=128)
    wo_r = wo.rearrange("(c p) n -> p c n", p=128)
    gwt_r = gwt.rearrange("(c p) n -> p c n", p=128)
    shg_r = shg_t.rearrange("(c p) n -> p c n", p=128)
    shu_r = shu_t.rearrange("(c p) n -> p c n", p=128)
    out_r = out.rearrange("(t p) d -> p t d", p=128)
    ctx_rd = ctx_dram.rearrange("h q p n -> p q h n")

    def kfm(h, a, w):
        return KV2[:, h * 2048 + a: h * 2048 + a + w]

    def vsl(kt, off, w):
        return KV2[:, 8192 + kt * 512 + off: 8192 + kt * 512 + off + w]

    def h2sl(d, a, w):
        return KV2[:, d * 1024 + a: d * 1024 + a + w]

    trst = {"g": 0, "ev": {}}

    def tr_batch(pl, ins, out_ap, eng="dv", pre_pe_ev=None):
        g = trst["g"]
        trst["g"] += 1
        psb = PSB6 if g % 2 == 0 else PSB7
        if g >= 2:
            pl.wait("pe", trst["ev"][g - 2])
        if pre_pe_ev is not None:
            pl.wait("pe", pre_pe_ev)
        ev = None
        n = len(ins)
        for i, src in enumerate(ins):
            fn = (lambda e, o=psb[:, i * 128:(i + 1) * 128], s=src:
                  e.transpose(o, s, IDT[:]))
            ev = pl.op("pe", fn, inc="pe" if i == n - 1 else None)
        pl.wait(eng, ev)
        if eng == "ac":
            evc = pl.op(eng, lambda e, o=out_ap, i=psb[:, 0:128 * n]: e.copy(o, i),
                        inc=eng)
        else:
            evc = pl.op(eng, lambda e, o=out_ap, i=psb[:, 0:128 * n]:
                        e.tensor_copy(o, i), inc=eng)
        trst["ev"][g] = evc
        return evc

    esA = ExitStack()

    def sbA(name, shape, dt):
        return esA.enter_context(nc.sbuf_tensor(name, shape, dt))

    H1 = sbA("H1", [128, NSLOT * 2048], BF16)
    COSF = sbA("COSF", [128, S], BF16)
    SINF = sbA("SINF", [128, S], BF16)
    WQB = sbA("WQB", [128, 16, 512], BF16)
    WKB = sbA("WKB", [128, 16, 512], BF16)
    WVB = sbA("WVB", [128, 16, 512], BF16)
    QF = sbA("QF", [128, 4096], BF16)
    ATT = sbA("ATT", [128, 3 * 2048], BF16)
    AT2 = sbA("AT2", [128, 2048], BF16)
    ESB = sbA("ESB", [128, 2048], F32)
    HTM = sbA("HTM", [128, 4096], BF16)
    TMS = sbA("TMS", [128, 512], BF16)
    TMS3 = sbA("TMS3", [128, 512], BF16)
    TMS4 = sbA("TMS4", [128, 512], BF16)
    CTXS = sbA("CTXS", [128, 256], BF16)

    p0 = Plan(sems, cnt)
    ev_const = None
    for dst, src in ((IDT[:], idt_in), (IDTF[:], idtf_in), (PSW[:], psw_in),
                     (TRI[:], tri_in), (TRI01[:], tri01_in),
                     (ONESB[:], onesb_in), (FLG[:], flg_in), (ONESF[:], onesf_in),
                     (ONESR[:], onesr_in), (COSF[:], cosf_in), (SINF[:], sinf_in),
                     (GWTB[:], gwt_r), (CVEC[:], cvec_in)):
        ev_const = p0.dma("gp", dst, src, "qW")
    p0.wait("gp", ev_const)
    ev_wg0 = ev_const
    for wbuf, wsrc in ((WQB, wq_r), (WKB, wk_r), (WVB, wv_r)):
        ev_wg0 = p0.dma("gp", wbuf[:], wsrc[:, :, 0:512], "qW")

    p0.wait("ac", ev_const)
    p0.wait("dv", ev_const)
    ev_x = [None] * NSLOT
    ev_sq = [None] * NSLOT
    ev_norm = [None] * NSLOT
    ev_h1 = [None] * NSLOT
    for kt in range(NSLOT):
        par = kt % 2
        xt = ESB[:, 0:2048]
        htm = HTM[:, par * 2048:(par + 1) * 2048]
        if kt >= 1:
            p0.wait("sy", ev_sq[kt - 1])
            p0.wait("sy", ev_norm[kt - 1])
        ev_x[kt] = p0.dma("sy", xt, x_r[:, kt, :], "qX")
        p0.wait("dv", ev_x[kt])
        p0.opw("dv", lambda e, i=xt, kt=kt: e.reduce_sum(
            STAT[:, kt:kt + 1], i, axis=AX))
        p0.wait("ac", ev_x[kt])
        ev_sq[kt] = p0.op("ac", lambda e, i=xt, kt=kt: e.activation(
            ATT[:, 0:2048], i, AF.Square, bias=0.0, scale=1.0,
            accum_out=STAT[:, 16 + kt:17 + kt]), inc="ac")
        p0.wait("ac", ev_sq[kt])
        p0.wait("dv", ev_sq[kt])
        p0.opw("dv", lambda e, kt=kt: e.tensor_scalar_mul(
            STAT[:, 32 + kt:33 + kt], STAT[:, kt:kt + 1], -1.0 / D))
        ev_pre = p0.opw("dv", lambda e, kt=kt: e.tensor_scalar_mul(
            STAT[:, 48 + kt:49 + kt], STAT[:, 16 + kt:17 + kt], 1.0 / D))
        p0.wait("ac", ev_pre)
        ev_mu2 = p0.opw("ac", lambda e, kt=kt: e.activation(
            STAT[:, 64 + kt:65 + kt], STAT[:, 32 + kt:33 + kt], AF.Square))
        p0.wait("dv", ev_mu2)
        ev_var = p0.opw("dv", lambda e, kt=kt: e.tensor_sub(
            STAT[:, 64 + kt:65 + kt], STAT[:, 48 + kt:49 + kt],
            STAT[:, 64 + kt:65 + kt]))
        p0.wait("ac", ev_var)
        p0.opw("ac", lambda e, kt=kt: e.activation(
            STAT[:, 64 + kt:65 + kt], STAT[:, 64 + kt:65 + kt], AF.Ln, bias=EPS))
        ev_rstd = p0.opw("ac", lambda e, kt=kt: e.activation(
            STAT[:, 64 + kt:65 + kt], STAT[:, 64 + kt:65 + kt], AF.Exp,
            scale=-0.5))
        p0.wait("dv", ev_rstd)
        if kt >= 2:
            p0.wait("dv", ev_h1[kt - 2])
        ev_norm[kt] = p0.op("dv", lambda e, i=xt, o=htm, kt=kt: e.tensor_scalar(
            out=o, in0=i, scalar1=STAT[:, 32 + kt:33 + kt],
            scalar2=STAT[:, 64 + kt:65 + kt], op0=ALU.add, op1=ALU.mult), inc="dv")
        for half in range(2):
            dst = bass.AP(H1, (half * 8) * 2048 + kt * 128,
                          [[NSLOT * 2048, 128], [2048, 8], [1, 128]])
            ins = [htm[:, (half * 8 + dc) * 128:(half * 8 + dc + 1) * 128]
                   for dc in range(8)]
            ev_h1[kt] = tr_batch(p0, ins, dst, eng="dv",
                                 pre_pe_ev=(ev_norm[kt] if half == 0 else None))
            if kt == 0 and half == 0:
                pass
    if "d_h1" in tap_t:
        p0.wait("sy", ev_h1[NSLOT - 1])
        p0.dma("sy", tap_t["d_h1"], H1[:], "qS")
    if "d_stat" in tap_t:
        p0.wait("sy", ev_h1[NSLOT - 1])
        p0.dma("sy", tap_t["d_stat"], STAT[:], "qS")
    p0.emit(nc)

    ev_spill_all = None
    if stage >= 2:
        p1 = Plan(sems, cnt)
        rs = {"ev_exp_row": None, "ev_ctx_row": None, "ev_ctxev_row": None,
              "pq": []}
        ev_trrow = {}
        ev_spill = {}
        ev_projr = None
        row = 0

        def emit_tail(p1):
            if not rs["pq"]:
                return
            pd = rs["pq"].pop(0)
            h, hh, qt, row_, kn = pd["h"], pd["hh"], pd["qt"], pd["row"], pd["kn"]
            attn, ev_nrm = pd["attn"], pd["ev_nrm"]
            nb = (kn + 7) // 8
            evb = []
            for b_ in range(nb):
                n_t = min(8, kn - b_ * 8)
                ins = [attn[:, (b_ * 8 + t) * 128:(b_ * 8 + t + 1) * 128]
                       for t in range(n_t)]
                eng = "dv"
                if b_ == 0 and rs["ev_ctx_row"] is not None:
                    p1.wait(eng, rs["ev_ctx_row"])
                evb.append(tr_batch(
                    p1, ins, AT2[:, b_ * 1024: b_ * 1024 + n_t * 128],
                    eng=eng, pre_pe_ev=(ev_nrm if b_ == 0 else None)))
            ev_trrow[row_] = ("pe", cnt["pe"])
            p1.wait("pe", rs["ev_ctxev_row"])
            for b_ in range(nb):
                p1.wait("pe", evb[b_])
                n_t = min(8, kn - b_ * 8)
                for t in range(n_t):
                    kt = b_ * 8 + t
                    rs["ev_ctx_row"] = p1.op("pe",
                        lambda e, kt=kt, h=h, kn=kn: e.matmul(
                            PS4[:, 0:128], vsl(kt, h * 128, 128),
                            AT2[:, kt * 128:(kt + 1) * 128],
                            start=(kt == 0), stop=(kt == kn - 1)),
                        inc="pe" if kt == kn - 1 else None)
            cpar = row_ % 2
            p1.wait("dv", rs["ev_ctx_row"])
            if row_ >= 2:
                p1.wait("dv", ev_spill[row_ - 2])
            rs["ev_ctxev_row"] = p1.op("dv", lambda e, cpar=cpar: e.tensor_copy(
                CTXS[:, cpar * 128:(cpar + 1) * 128], PS4[:, 0:128]),
                inc="dv")
            p1.wait("sy", rs["ev_ctxev_row"])
            ev_spill[row_] = p1.dma(
                "sy", ctx_dram[hh, qt, :, :],
                CTXS[:, cpar * 128:(cpar + 1) * 128], "qS")

        for g in range(4):
            if g >= 1:
                p1.wait("gp", ev_projr)
                ev_w = None
                for wbuf, wsrc in ((WQB, wq_r), (WKB, wk_r), (WVB, wv_r)):
                    ev_w = p1.dma("gp", wbuf[:], wsrc[:, :, g * 512:(g + 1) * 512], "qW")
                ev_wcur = ev_w
            else:
                ev_wcur = ev_wg0

            emit_tail(p1)
            emit_tail(p1)

            unit = 0
            last_bank_ev = {0: None, 1: None}

            def bank(u):
                return (PS4, 0) if u % 2 == 0 else (PS5, 1)

            if rs["ev_ctxev_row"] is not None:
                p1.wait("pe", rs["ev_ctxev_row"])
            p1.wait("pe", ev_wcur)
            if rs["ev_ctx_row"] is not None:
                p1.wait("ac", rs["ev_ctx_row"])
                p1.wait("dv", rs["ev_ctx_row"])
            for kt in range(NSLOT):
                pb, bi = bank(unit)
                if last_bank_ev[bi] is not None:
                    p1.wait("pe", last_bank_ev[bi])
                ev_mm = None
                for d_ in range(16):
                    ev_mm = p1.op("pe", lambda e, pb=pb, d=d_, kt=kt: e.matmul(
                        pb[:, 0:512],
                        H1[:, d * 2048 + kt * 128: d * 2048 + kt * 128 + 128],
                        WVB[:, d, :], start=(d == 0), stop=(d == 15)),
                        inc="pe" if d_ == 15 else None)
                p1.wait("ac", ev_mm)
                evv = p1.op("ac", lambda e, pb=pb, kt=kt: e.copy(
                    vsl(kt, 0, 512), pb[:, 0:512]), inc="ac")
                last_bank_ev[bi] = evv
                unit += 1
            ev_t3 = None
            ev_radd_prev = None
            ev_sw_prev = None
            for typ in ("k", "q"):
                wb = WKB if typ == "k" else WQB
                ncc = 4 if typ == "k" else 2
                base = 0 if typ == "k" else 1024
                for h in range(4):
                    for cc in range(ncc):
                        pb, bi = bank(unit)
                        if last_bank_ev[bi] is not None:
                            p1.wait("pe", last_bank_ev[bi])
                        ev_mm = None
                        for d_ in range(16):
                            ev_mm = p1.op("pe",
                                lambda e, pb=pb, d=d_, h=h, cc=cc, wb=wb, base=base: e.matmul(
                                    pb[:, 0:512],
                                    wb[:, d, h * 128:(h + 1) * 128],
                                    H1[:, d * 2048 + base + cc * 512:
                                       d * 2048 + base + (cc + 1) * 512],
                                    start=(d == 0), stop=(d == 15)),
                                inc="pe" if d_ == 15 else None)
                        p1.wait("ac", ev_mm)
                        if ev_t3 is not None:
                            p1.wait("ac", ev_t3)
                        if ev_sw_prev is not None:
                            p1.wait("ac", ev_sw_prev)
                        ev_ev = p1.op("ac", lambda e, pb=pb: e.copy(
                            TMS[:], pb[:, 0:512]), inc="ac")
                        p1.wait("dv", ev_ev)
                        if ev_radd_prev is not None:
                            p1.wait("dv", ev_radd_prev)
                        ev_t3 = p1.opw("dv", lambda e, cc=cc, base=base: e.tensor_mul(
                            TMS3[:], TMS[:],
                            COSF[:, base + cc * 512: base + (cc + 1) * 512]))
                        p1.wait("pe", ev_ev)
                        ev_sw = p1.op("pe", lambda e, pb=pb: e.matmul(
                            pb[:, 0:512], PSW[:], TMS[:], start=True, stop=True),
                            inc="pe")
                        p1.wait("dv", ev_sw)
                        p1.opw("dv", lambda e, pb=pb, cc=cc, base=base: e.tensor_mul(
                            TMS4[:], pb[:, 0:512],
                            SINF[:, base + cc * 512: base + (cc + 1) * 512]))
                        if typ == "k":
                            dstap = kfm(h, cc * 512, 512)
                        else:
                            dstap = QF[:, h * 1024 + cc * 512: h * 1024 + (cc + 1) * 512]
                        evr = p1.op("dv", lambda e, o=dstap: e.tensor_add(
                            o, TMS3[:], TMS4[:]), inc="dv")
                        ev_radd_prev = evr
                        ev_sw_prev = ev_sw
                        last_bank_ev[bi] = evr
                        unit += 1
            ev_projr = ev_mm

            ev_bias = None
            for h_ in range(4):
                if h_ == 0:
                    p1.wait("dv", rs["ev_ctx_row"])
                ev_q2 = p1.op("dv", lambda e, h=h_: e.tensor_mul(
                    AT2[:, 0:1024], QF[:, h * 1024:(h + 1) * 1024],
                    QF[:, h * 1024:(h + 1) * 1024]), inc="dv")
                p1.wait("pe", ev_q2)
                if h_ == 0:
                    p1.wait("pe", rs["ev_exp_row"])
                ev_qm = None
                for c_ in range(2):
                    ev_qm = p1.op("pe", lambda e, c=c_: e.matmul(
                        PSBIG[0:1, c * 512:(c + 1) * 512], ONESB[:],
                        AT2[:, c * 512:(c + 1) * 512],
                        start=True, stop=True), inc="pe" if c_ == 1 else None)
                p1.wait("dv", ev_qm)
                p1.opw("dv", lambda e: e.reduce_max(
                    STAT[0:1, 92:93], PSBIG[0:1, 0:1024], axis=AX))
                ev_k2 = p1.op("dv", lambda e, h=h_: e.tensor_mul(
                    AT2[:, 0:2048], kfm(h, 0, 2048), kfm(h, 0, 2048)),
                    inc="dv")
                p1.wait("pe", ev_k2)
                ev_km = None
                for c_ in range(4):
                    ev_km = p1.op("pe", lambda e, c=c_: e.matmul(
                        PSBIG[0:1, c * 512:(c + 1) * 512], ONESB[:],
                        AT2[:, c * 512:(c + 1) * 512],
                        start=True, stop=True), inc="pe" if c_ == 3 else None)
                p1.wait("dv", ev_km)
                p1.opw("dv", lambda e: e.reduce_max(
                    STAT[0:1, 93:94], PSBIG[0:1, 0:2048], axis=AX))
                ev_prod = p1.opw("dv", lambda e: e.tensor_mul(
                    STAT[0:1, 92:93], STAT[0:1, 92:93], STAT[0:1, 93:94]))
                p1.wait("ac", ev_prod)
                ev_sqr = p1.opw("ac", lambda e: e.activation(
                    STAT[0:1, 92:93], STAT[0:1, 92:93], AF.Sqrt))
                p1.wait("dv", ev_sqr)
                ev_sc2 = p1.opw("dv", lambda e: e.tensor_scalar_mul(
                    STAT[0:1, 92:93], STAT[0:1, 92:93], -1.02 * INV_SQRT_HD))
                p1.wait("pe", ev_sc2)
                if h_ == 0:
                    p1.wait("pe", rs["ev_ctxev_row"])
                ev_bc = p1.op("pe", lambda e: e.matmul(
                    PS4[:, 0:1], ONESR[:], STAT[0:1, 92:93],
                    start=True, stop=True), inc="pe")
                p1.wait("ac", ev_bc)
                ev_bias = p1.op("ac", lambda e, h=h_: e.copy(
                    STAT[:, 88 + h:89 + h], PS4[:, 0:1]), inc="ac")
            rs["ev_ctxev_row"] = ev_bias

            if g == 3 and "d_kfm" in tap_t:
                p1.wait("sy", last_bank_ev[0])
                p1.wait("sy", last_bank_ev[1])
                p1.dma("sy", tap_t["d_kfm"], KV2[:, 0:8192], "qS")
                p1.dma("sy", tap_t["d_qfm"], QF[:], "qS")
                p1.dma("sy", tap_t["d_vsb"], KV2[:, 8192:16384], "qS")

            for h in range(4):
                hh = g * 4 + h
                for qt in range(NT):
                    col = hh * 8 + qt
                    kn = 9 + qt
                    kctx = kn * 128
                    par = row % 3
                    attn = ATT[:, par * 2048: par * 2048 + 2048]
                    p1.wait("pe", rs["ev_exp_row"])
                    nch = (kctx + 511) // 512
                    ev_sc = None
                    for c in range(nch):
                        w = min(512, kctx - c * 512)
                        ev_sc = p1.op("pe", lambda e, c=c, w=w, h=h, qt=qt: e.matmul(
                            PSBIG[:, c * 512: c * 512 + w],
                            QF[:, h * 1024 + qt * 128: h * 1024 + (qt + 1) * 128],
                            kfm(h, c * 512, w), start=True, stop=True),
                            inc="pe" if c == nch - 1 else None)
                    if len(rs["pq"]) >= 2:
                        emit_tail(p1)
                    p1.wait("ac", ev_sc)
                    if row >= 3:
                        p1.wait("ac", ev_trrow[row - 3])
                    p1.op("ac", lambda e, attn=attn, col=col, h=h: e.activation(
                        attn[:, 0:1024], PSBIG[:, 0:1024], AF.Exp,
                        bias=STAT[:, 88 + h:89 + h], scale=INV_SQRT_HD,
                        accum_out=GRID[:, col:col + 1]))
                    if kn > 9:
                        p1.op("ac", lambda e, attn=attn, col=col, kn=kn, h=h:
                            e.activation(
                                attn[:, 1024:(kn - 1) * 128],
                                PSBIG[:, 1024:(kn - 1) * 128], AF.Exp,
                                bias=STAT[:, 88 + h:89 + h], scale=INV_SQRT_HD,
                                accum_out=GRID[:, 128 + col:129 + col]))
                    ev_exp = p1.op("ac", lambda e, attn=attn, kn=kn, h=h:
                        e.activation(
                            attn[:, (kn - 1) * 128: kn * 128],
                            PSBIG[:, (kn - 1) * 128: kn * 128], AF.Exp,
                            bias=STAT[:, 88 + h:89 + h], scale=INV_SQRT_HD),
                        inc="ac")
                    rs["ev_exp_row"] = ev_exp
                    p1.wait("dv", ev_exp)
                    p1.opw("dv", lambda e, attn=attn, kn=kn: e.tensor_mul(
                        attn[:, (kn - 1) * 128: kn * 128],
                        attn[:, (kn - 1) * 128: kn * 128], TRI01[:]))
                    p1.opw("dv", lambda e, attn=attn, kn=kn: e.reduce_sum(
                        STAT[:, 84:85], attn[:, (kn - 1) * 128: kn * 128],
                        axis=AX))
                    if kn > 9:
                        p1.opw("dv", lambda e, col=col: e.scalar_tensor_tensor(
                            out=STAT[:, 82:83], in0=GRID[:, col:col + 1],
                            scalar=FLG[:, 0:1], in1=GRID[:, 128 + col:129 + col],
                            op0=ALU.mult, op1=ALU.add))
                    else:
                        p1.opw("dv", lambda e, col=col: e.tensor_scalar(
                            out=STAT[:, 82:83], in0=GRID[:, col:col + 1],
                            scalar1=FLG[:, 0:1], scalar2=None, op0=ALU.mult))
                    p1.opw("dv", lambda e: e.tensor_add(
                        STAT[:, 82:83], STAT[:, 82:83], STAT[:, 84:85]))
                    p1.opw("dv", lambda e: e.reciprocal(STAT[:, 83:84], STAT[:, 82:83]))
                    ev_nrm = p1.op("dv", lambda e, attn=attn, kctx=kctx:
                        e.tensor_scalar_mul(
                            attn[:, 0:kctx], attn[:, 0:kctx], STAT[:, 83:84]),
                        inc="dv")
                    rs["pq"].append({"h": h, "hh": hh, "qt": qt, "row": row,
                                     "kn": kn, "attn": attn, "ev_nrm": ev_nrm})
                    row += 1
        emit_tail(p1)
        emit_tail(p1)
        p1.emit(nc)
        ev_spill_all = ("qS", cnt["qS"])

    esA.close()

    ev_x2st = [None] * NT
    if stage >= 3:
        esB = ExitStack()

        def sbB(name, shape, dt):
            return esB.enter_context(nc.sbuf_tensor(name, shape, dt))

        WOB = sbB("WOB", [128, 16, 2048], BF16)
        CTXB = sbB("CTXB", [128, 2, 16, 128], BF16)
        XT = sbB("XT", [128, 2, 2048], F32)
        XNEW = sbB("XNEW", [128, 2048], F32)
        XFM = sbB("XFM", [128, 2048], F32)
        HT2 = sbB("HT2", [128, 2048], BF16)
        SQ2 = sbB("SQ2", [128, 2048], BF16)
        LG = sbB("LG", [128, 64], F32)

        p2 = Plan(sems, cnt)
        ev_wo = None
        for hh in range(16):
            ev_wo = p2.dma("gp", WOB[:, hh, :], wo_r[:, hh, :], "qW")
        ev_sp = None
        for d_ in range(16):
            ev_sp = p2.op("pe", lambda e, d=d_: e.matmul(
                PS5[0:1, 0:8], ONESF[:], GWTB[:, d, :],
                start=(d == 0), stop=(d == 15)), inc="pe" if d_ == 15 else None)
        p2.wait("dv", ev_sp)
        ev_srv = p2.opw("dv", lambda e: e.tensor_copy(SRV[:], PS5[0:1, 0:8]))
        p2.wait("pe", ev_srv)
        ev_sbp = p2.op("pe", lambda e: e.matmul(
            PS5[:, 16:24], ONESR[:], SRV[:], start=True, stop=True), inc="pe")
        p2.wait("dv", ev_sbp)
        ev_sbc = p2.opw("dv", lambda e: e.tensor_copy(SBC[:], PS5[:, 16:24]))

        ev_ctxb = [None, None]
        ev_xt = [None, None]
        ev_wolast = [None, None]
        ev_xtfree = [None, None]
        ev_ep = {0: ev_sbc, 1: ev_sbc}
        ev_gtr = None
        ev_s2prev = None
        ev_xfm_ev = None
        ev_h2w = None
        ev_wm = None
        for qt in range(NT):
            par = qt % 2
            if qt >= 2:
                p2.wait("sy", ev_wolast[par])
                p2.wait("sy", ev_xtfree[par])
            if qt == 0:
                p2.wait("sy", ev_spill_all)
            ev_ctxb[par] = p2.dma("sy", CTXB[:, par, :, :], ctx_rd[:, qt, :, :], "qX")
            ev_xt[par] = p2.dma("sy", XT[:, par, :], x_r[:, 8 + qt, :], "qX")
            for dc in range(4):
                pb = PS4 if dc % 2 == 0 else PS5
                p2.wait("pe", ev_ctxb[par])
                if qt == 0 and dc == 0:
                    p2.wait("pe", ev_wo)
                p2.wait("pe", ev_ep[dc % 2])
                ev_mm = None
                for hh in range(16):
                    ev_mm = p2.op("pe",
                        lambda e, pb=pb, hh=hh, dc=dc, par=par: e.matmul(
                            pb[:, 0:512], CTXB[:, par, hh, :],
                            WOB[:, hh, dc * 512:(dc + 1) * 512],
                            start=(hh == 0), stop=(hh == 15)),
                        inc="pe" if hh == 15 else None)
                if dc == 3:
                    ev_wolast[par] = ev_mm
                p2.wait("dv", ev_mm)
                p2.wait("dv", ev_xt[par])
                if dc == 0 and qt > 0:
                    p2.wait("dv", ev_gtr)
                    p2.wait("dv", ev_s2prev)
                    p2.wait("dv", ev_x2st[qt - 1])
                ev_ep[dc % 2] = p2.op("dv",
                    lambda e, pb=pb, dc=dc, par=par: e.tensor_add(
                        XNEW[:, dc * 512:(dc + 1) * 512], pb[:, 0:512],
                        XT[:, par, dc * 512:(dc + 1) * 512]), inc="dv")
                if dc == 3:
                    ev_xtfree[par] = ev_ep[1]
            ev_xn = ev_ep[1]
            p2.wait("sy", ev_xn)
            ev_x2st[qt] = p2.dma("sy", out_r[:, qt, :], XNEW[:], "qS")
            p2.wait("dv", ev_xn)
            p2.opw("dv", lambda e, qt=qt: e.reduce_sum(
                STAT[:, qt:qt + 1], XNEW[:], axis=AX))
            p2.wait("ac", ev_xn)
            ev_s2 = p2.op("ac", lambda e, qt=qt: e.activation(
                SQ2[:], XNEW[:], AF.Square, bias=0.0, scale=1.0,
                accum_out=STAT[:, 16 + qt:17 + qt]), inc="ac")
            p2.wait("ac", ev_s2)
            ev_s2prev = ev_s2
            p2.wait("dv", ev_s2)
            p2.opw("dv", lambda e, qt=qt: e.tensor_scalar_mul(
                STAT[:, 32 + qt:33 + qt], STAT[:, qt:qt + 1], -1.0 / D))
            ev_pre = p2.opw("dv", lambda e, qt=qt: e.tensor_scalar_mul(
                STAT[:, 48 + qt:49 + qt], STAT[:, 16 + qt:17 + qt], 1.0 / D))
            p2.wait("ac", ev_pre)
            ev_mu2 = p2.opw("ac", lambda e, qt=qt: e.activation(
                STAT[:, 64 + qt:65 + qt], STAT[:, 32 + qt:33 + qt], AF.Square))
            p2.wait("dv", ev_mu2)
            ev_var = p2.opw("dv", lambda e, qt=qt: e.tensor_sub(
                STAT[:, 64 + qt:65 + qt], STAT[:, 48 + qt:49 + qt],
                STAT[:, 64 + qt:65 + qt]))
            p2.wait("ac", ev_var)
            p2.opw("ac", lambda e, qt=qt: e.activation(
                STAT[:, 64 + qt:65 + qt], STAT[:, 64 + qt:65 + qt], AF.Ln, bias=EPS))
            ev_rstd = p2.opw("ac", lambda e, qt=qt: e.activation(
                STAT[:, 64 + qt:65 + qt], STAT[:, 64 + qt:65 + qt], AF.Exp,
                scale=-0.5))
            p2.wait("dv", ev_rstd)
            if ev_h2w is not None:
                p2.wait("dv", ev_h2w)
            ev_n2 = p2.op("dv", lambda e, qt=qt: e.tensor_scalar(
                out=HT2[:], in0=XNEW[:], scalar1=STAT[:, 32 + qt:33 + qt],
                scalar2=STAT[:, 64 + qt:65 + qt], op0=ALU.add, op1=ALU.mult),
                inc="dv")
            for half in range(2):
                dst = bass.AP(KV2, (half * 8) * 1024 + qt * 128,
                              [[16384, 128], [1024, 8], [1, 128]])
                ins = [HT2[:, (half * 8 + dc) * 128:(half * 8 + dc + 1) * 128]
                       for dc in range(8)]
                ev_h2w = tr_batch(p2, ins, dst, eng="dv",
                                  pre_pe_ev=(ev_n2 if half == 0 else None))
            if ev_xfm_ev is not None:
                p2.wait("pe", ev_xfm_ev)
            ev_gt = None
            for dd in range(16):
                ev_gt = p2.op("pe", lambda e, dd=dd: e.transpose(
                    PSBIG[:, dd * 128:(dd + 1) * 128],
                    XNEW[:, dd * 128:(dd + 1) * 128], IDTF[:]),
                    inc="pe" if dd == 15 else None)
            ev_gtr = ev_gt
            p2.wait("dv", ev_gt)
            ev_xfm_ev = p2.op("dv", lambda e: e.tensor_copy(XFM[:], PSBIG[:]),
                              inc="dv")
            p2.wait("pe", ev_ep[0])
            p2.wait("pe", ev_xfm_ev)
            ev_lgm = None
            for d_ in range(16):
                ev_lgm = p2.op("pe", lambda e, d=d_: e.matmul(
                    PS4[:, 0:8], XFM[:, d * 128:(d + 1) * 128], GWTB[:, d, :],
                    start=(d == 0), stop=(d == 15)), inc="pe" if d_ == 15 else None)
            p2.wait("dv", ev_lgm)
            p2.opw("dv", lambda e, qt=qt: e.scalar_tensor_tensor(
                out=LG[:, 0:8], in0=SBC[:], scalar=STAT[:, 32 + qt:33 + qt],
                in1=PS4[:, 0:8], op0=ALU.mult, op1=ALU.add))
            ev_lg = p2.opw("dv", lambda e, qt=qt: e.tensor_scalar_mul(
                LG[:, 0:8], LG[:, 0:8], STAT[:, 64 + qt:65 + qt]))
            ev_ep[0] = ev_lg
            ev_rm = p2.opw("dv", lambda e: e.reduce_max(
                STAT[:, 84:85], LG[:, 0:8], axis=AX, negate=True))
            p2.wait("ac", ev_rm)
            ev_sm = p2.op("ac", lambda e: e.activation(
                LG[:, 8:16], LG[:, 0:8], AF.Exp, bias=STAT[:, 84:85],
                accum_out=STAT[:, 85:86]), inc="ac")
            p2.wait("ac", ev_sm)
            p2.wait("dv", ev_sm)
            p2.opw("dv", lambda e: e.reciprocal(STAT[:, 86:87], STAT[:, 85:86]))
            p2.opw("dv", lambda e: e.tensor_scalar_mul(
                LG[:, 8:16], LG[:, 8:16], STAT[:, 86:87]))
            sc = LG[:, 8:16].rearrange("p (a s) -> p a s", s=2)
            p2.opw("dv", lambda e, sc=sc: e.tensor_max(
                LG[:, 16:20], sc[:, :, 0], sc[:, :, 1]))
            p2.opw("dv", lambda e: e.reduce_max(STAT[:, 87:88], LG[:, 16:20], axis=AX))
            p2.opw("dv", lambda e: e.tensor_scalar(
                out=LG[:, 20:24], in0=LG[:, 16:20], scalar1=STAT[:, 87:88],
                scalar2=None, op0=ALU.is_equal))
            p2.opw("dv", lambda e: e.scalar_tensor_tensor(
                out=LG[:, 24:28], in0=LG[:, 20:24], scalar=-1.0e30,
                in1=LG[:, 16:20], op0=ALU.mult, op1=ALU.add))
            p2.opw("dv", lambda e: e.reduce_max(STAT[:, 88:89], LG[:, 24:28], axis=AX))
            p2.opw("dv", lambda e: e.tensor_scalar(
                out=LG[:, 20:24], in0=LG[:, 16:20], scalar1=STAT[:, 88:89],
                scalar2=None, op0=ALU.is_ge))
            mk = LG[:, 32:40].rearrange("p (a s) -> p a s", s=2)
            p2.opw("dv", lambda e, sc=sc, mk=mk: e.tensor_mul(
                mk[:, :, 0], sc[:, :, 0], LG[:, 20:24]))
            p2.opw("dv", lambda e, sc=sc, mk=mk: e.tensor_mul(
                mk[:, :, 1], sc[:, :, 1], LG[:, 20:24]))
            p2.opw("dv", lambda e: e.reduce_max(STAT[:, 87:88], LG[:, 32:40], axis=AX))
            p2.opw("dv", lambda e: e.tensor_scalar(
                out=LG[:, 40:48], in0=LG[:, 32:40], scalar1=STAT[:, 87:88],
                scalar2=None, op0=ALU.is_equal))
            p2.opw("dv", lambda e: e.scalar_tensor_tensor(
                out=LG[:, 48:56], in0=LG[:, 40:48], scalar=-1.0e30,
                in1=LG[:, 32:40], op0=ALU.mult, op1=ALU.add))
            p2.opw("dv", lambda e: e.reduce_max(STAT[:, 88:89], LG[:, 48:56], axis=AX))
            p2.opw("dv", lambda e: e.tensor_scalar(
                out=LG[:, 40:48], in0=LG[:, 32:40], scalar1=STAT[:, 88:89],
                scalar2=None, op0=ALU.is_ge))
            ev_wm = p2.opw("dv", lambda e, qt=qt: e.tensor_mul(
                WMAT[:, qt * 8:(qt + 1) * 8], LG[:, 32:40], LG[:, 40:48]))
        if "d_h2" in tap_t:
            p2.wait("sy", ev_h2w)
            p2.dma("sy", tap_t["d_h2"], KV2[:], "qS")
        if "d_wmat" in tap_t:
            p2.wait("sy", ev_wm)
            p2.dma("sy", tap_t["d_wmat"], WMAT[:], "qS")
        if "d_logit" in tap_t:
            p2.wait("sy", ev_wm)
            p2.dma("sy", tap_t["d_logit"], WMAT[:], "qS")
        p2.emit(nc)
        esB.close()

    if stage >= 4:
        esC = ExitStack()

        def sbC(name, shape, dt):
            return esC.enter_context(nc.sbuf_tensor(name, shape, dt))

        GUB = sbC("GUB", [128, 3, 16, 384], BF16)
        UUB = sbC("UUB", [128, 3, 16, 384], BF16)
        DOWNB = sbC("DOWNB", [128, 11, 2048], BF16)
        ACTA = sbC("ACTA", [128, NT, FE], BF16)
        ACTT = sbC("ACTT", [128, 2, FE], BF16)
        SGB = sbC("SGB", [128, 512], BF16)
        ESB3 = sbC("ESB3", [128, 2048], F32)

        p3 = Plan(sems, cnt)
        FCS = ((0, 384), (384, 384), (768, 384), (1152, 256))
        experts = [("sh", 0), ("sh", 1)] + [("e", n) for n in range(NE)]
        gfc = 0
        ev_gu_cons = {}
        ev_down_done = None
        ev_tr_exp = None
        ev_store = {}
        ev_dn_tt = {}
        ev_evac_h = [None, None]
        ev_silu = None
        ev_stt = None
        for mi, (kind, idx) in enumerate(experts):
            def gu_src(off, w, kind=kind, idx=idx):
                if kind == "sh":
                    return (shg_r[:, :, idx * FE + off: idx * FE + off + w],
                            shu_r[:, :, idx * FE + off: idx * FE + off + w])
                return (gate_t[idx].rearrange("(c p) f -> p c f", p=128)[
                            :, :, off:off + w],
                        up_t[idx].rearrange("(c p) f -> p c f", p=128)[
                            :, :, off:off + w])

            def emit_gu_load(gfc_, off, w):
                gpar = gfc_ % 3
                p3.wait("gp", ev_gu_cons.get(gfc_ - 3))
                gsrc, usrc = gu_src(off, w)
                p3.dma("gp", GUB[:, gpar, :, 0:w], gsrc, "qW")
                return p3.dma("gp", UUB[:, gpar, :, 0:w], usrc, "qW")

            myfc = []
            for fci, (off, w) in enumerate(FCS):
                gpar = gfc % 3
                evu = emit_gu_load(gfc, off, w) if fci < 3 else None
                myfc.append([gfc, gpar, off, w, evu])
                gfc += 1
            p3.wait("gp", ev_down_done)
            ev_db = None
            for fc in range(11):
                if kind == "sh":
                    dsrc = shd_t[idx * FE + fc * 128: idx * FE + (fc + 1) * 128, :]
                else:
                    dsrc = down_t[idx][fc * 128:(fc + 1) * 128, :]
                ev_db = p3.dma("gp", DOWNB[:, fc, :], dsrc, "qW")
            for fci, rec in enumerate(myfc):
                (gfci, gpar, off, w, evu) = rec
                for tt in range(NT):
                    p3.wait("pe", evu)
                    if ev_silu is not None:
                        p3.wait("pe", ev_silu)
                    ev_g = None
                    for d_ in range(16):
                        ev_g = p3.op("pe",
                            lambda e, d=d_, tt=tt, gpar=gpar, w=w: e.matmul(
                                PS4[:, 0:w], h2sl(d, tt * 128, 128),
                                GUB[:, gpar, d, 0:w],
                                start=(d == 0), stop=(d == 15)),
                            inc="pe" if d_ == 15 else None)
                    p3.wait("ac", ev_g)
                    if ev_stt is not None:
                        p3.wait("ac", ev_stt)
                    ev_silu = p3.op("ac", lambda e, w=w: e.activation(
                        SGB[:, 0:w], PS4[:, 0:w], AF.Silu), inc="ac")
                    if ev_stt is not None:
                        p3.wait("pe", ev_stt)
                    ev_u = None
                    for d_ in range(16):
                        ev_u = p3.op("pe",
                            lambda e, d=d_, tt=tt, gpar=gpar, w=w: e.matmul(
                                PS5[:, 0:w], h2sl(d, tt * 128, 128),
                                UUB[:, gpar, d, 0:w],
                                start=(d == 0), stop=(d == 15)),
                            inc="pe" if d_ == 15 else None)
                    p3.wait("dv", ev_u)
                    p3.wait("dv", ev_silu)
                    if ev_tr_exp is not None:
                        p3.wait("dv", ev_tr_exp)
                    if kind == "sh":
                        ev_stt = p3.op("dv",
                            lambda e, tt=tt, off=off, w=w: e.scalar_tensor_tensor(
                                out=ACTA[:, tt, off:off + w], in0=PS5[:, 0:w],
                                scalar=1.0, in1=SGB[:, 0:w],
                                op0=ALU.mult, op1=ALU.mult), inc="dv")
                    else:
                        ev_stt = p3.op("dv",
                            lambda e, tt=tt, off=off, w=w, n=idx:
                            e.scalar_tensor_tensor(
                                out=ACTA[:, tt, off:off + w], in0=PS5[:, 0:w],
                                scalar=WMAT[:, tt * 8 + n: tt * 8 + n + 1],
                                in1=SGB[:, 0:w], op0=ALU.mult, op1=ALU.mult),
                            inc="dv")
                ev_gu_cons[gfci] = ev_stt
                if fci == 0:
                    off2, w2 = FCS[3]
                    myfc[3][4] = emit_gu_load(myfc[3][0], off2, w2)
            for tt in range(NT):
                gtt = mi * NT + tt
                apar = gtt % 2
                p3.wait("dv", ev_dn_tt.get(gtt - 2))
                p3.wait("ac", ev_dn_tt.get(gtt - 2))
                ev_t1 = tr_batch(p3, [ACTA[:, tt, c * 128:(c + 1) * 128]
                                      for c in range(8)],
                                 ACTT[:, apar, 0:1024], eng="dv",
                                 pre_pe_ev=(ev_stt if tt == 0 else None))
                ev_t2 = tr_batch(p3, [ACTA[:, tt, c * 128:(c + 1) * 128]
                                      for c in range(8, 11)],
                                 ACTT[:, apar, 1024:1408], eng="ac")
                if tt == NT - 1:
                    ev_tr_exp = ("pe", cnt["pe"])
                p3.wait("pe", ev_db)
                p3.wait("pe", ev_t1)
                p3.wait("pe", ev_t2)
                for half in range(2):
                    p3.wait("pe", ev_evac_h[half])
                    ev_dn = None
                    for fc in range(11):
                        for dc in range(2):
                            co = half * 1024 + dc * 512
                            ev_dn = p3.op("pe", lambda e, fc=fc, co=co, apar=apar:
                                e.matmul(
                                    PSBIG[:, co: co + 512],
                                    ACTT[:, apar, fc * 128:(fc + 1) * 128],
                                    DOWNB[:, fc, co: co + 512],
                                    start=(fc == 0), stop=(fc == 10)),
                                inc="pe" if (fc == 10 and dc == 1) else None)
                    eng = "dv" if half == 0 else "ac"
                    p3.wait(eng, ev_dn)
                    p3.wait(eng, ev_store.get((tt, half)))
                    hsl = slice(half * 1024, (half + 1) * 1024)
                    if eng == "dv":
                        ev_evac_h[half] = p3.op(eng, lambda e, hsl=hsl:
                            e.tensor_copy(ESB3[:, hsl], PSBIG[:, hsl]), inc=eng)
                    else:
                        ev_evac_h[half] = p3.op(eng, lambda e, hsl=hsl:
                            e.copy(ESB3[:, hsl], PSBIG[:, hsl]), inc=eng)
                    p3.wait("gp", ev_evac_h[half])
                    p3.wait("gp", ev_store.get((tt, half)))
                    if mi == 0:
                        p3.wait("gp", ev_x2st[tt])
                    ev_store[(tt, half)] = p3.dma(
                        "gp", out_r[:, tt, hsl], ESB3[:, hsl], "qA", accum=True)
                ev_dn_tt[gtt] = ev_dn
                ev_down_done = ev_dn
        if "d_acta" in tap_t:
            p3.wait("sy", ev_stt)
            p3.dma("sy", tap_t["d_acta"], ACTA[:], "qS")
        p3.op("gp", lambda e: e.engine_nop())
        p3.wait("gp", ("qA", cnt["qA"]))
        p3.emit(nc)
        esC.close()

    esP.close()
    return nc, tap_t


def _rope_tables(j):
    if j == 1:
        pos = np.arange(S, dtype=np.float32)
    else:
        pos = np.concatenate([np.zeros(1024, np.float32),
                              np.arange(1024, dtype=np.float32)])
    freqs = 1.0 / 10000.0 ** (np.arange(0, HD, 2, dtype=np.float32) / HD)
    ang = np.outer(pos, freqs)
    c = np.cos(ang).T.astype(np.float32)
    s_ = np.sin(ang).T.astype(np.float32)
    cosf = np.concatenate([c, c], axis=0)
    sinf = np.concatenate([-s_, s_], axis=0)
    return np.ascontiguousarray(cosf), np.ascontiguousarray(sinf)


def stage_inputs(inputs):
    x = np.asarray(inputs["L0_attn_x"], np.float32)
    perm = np.concatenate([np.arange(0, HD, 2), np.arange(1, HD, 2)])
    permf = np.concatenate([h * HD + perm for h in range(H)])
    wq_p = np.ascontiguousarray(np.asarray(inputs["L0_attn_Wq"], np.float32)[:, permf])
    wk_p = np.ascontiguousarray(np.asarray(inputs["L0_attn_Wk"], np.float32)[:, permf])
    wv = np.ascontiguousarray(np.asarray(inputs["L0_attn_Wv"], np.float32))
    wo = np.ascontiguousarray(np.asarray(inputs["L0_attn_Wo"], np.float32))
    gwt = np.ascontiguousarray(np.asarray(inputs["L0_ffn_gate_weight"], np.float32).T)
    gate_t = np.ascontiguousarray(
        np.asarray(inputs["L0_ffn_gate_w"], np.float32).transpose(0, 2, 1))
    up_t = np.ascontiguousarray(
        np.asarray(inputs["L0_ffn_up_w"], np.float32).transpose(0, 2, 1))
    down_t = np.ascontiguousarray(
        np.asarray(inputs["L0_ffn_down_w"], np.float32).transpose(0, 2, 1))
    shg_t = np.ascontiguousarray(np.asarray(inputs["L0_ffn_sh_gate_w"], np.float32).T)
    shu_t = np.ascontiguousarray(np.asarray(inputs["L0_ffn_sh_up_w"], np.float32).T)
    shd_t = np.ascontiguousarray(np.asarray(inputs["L0_ffn_sh_down_w"], np.float32).T)

    idt = np.eye(128, dtype=np.float32).astype(ml_dtypes.bfloat16)
    idtf = np.eye(128, dtype=np.float32)
    pswap = np.zeros((128, 128), np.float32)
    for i in range(64):
        pswap[i, i + 64] = 1.0
        pswap[i + 64, i] = 1.0
    pswap = pswap.astype(ml_dtypes.bfloat16)
    tri = np.where(np.arange(128)[None, :] > np.arange(128)[:, None],
                   np.float32(-1.0e30), np.float32(0.0)).astype(np.float32)
    tri01 = np.where(np.arange(128)[None, :] > np.arange(128)[:, None],
                     np.float32(0.0), np.float32(1.0)).astype(ml_dtypes.bfloat16)
    onesb = np.ones((128, 1), ml_dtypes.bfloat16)
    onesf = np.ones((128, 1), np.float32)
    onesr = np.ones((1, 128), np.float32)

    in_maps = []
    for c in range(8):
        b, j = divmod(c, 2)
        xs = np.zeros((S, D), np.float32)
        if j == 1:
            xs[:] = x[b]
        else:
            xs[1024:] = x[b, 0:1024]
        cosf, sinf = _rope_tables(j)
        flg = np.full((128, 1), float(j), np.float32)
        in_maps.append({
            "x": xs, "wq": wq_p, "wk": wk_p, "wv": wv, "wo": wo, "gwt": gwt,
            "gate_t": gate_t, "up_t": up_t, "down_t": down_t,
            "shg_t": shg_t, "shu_t": shu_t, "shd_t": shd_t,
            "cosf": cosf, "sinf": sinf, "idt": idt, "idtf": idtf,
            "psw": pswap, "tri": tri, "tri01": tri01, "onesb": onesb,
            "flg": flg,
            "onesf": onesf, "onesr": onesr,
            "cvec": np.broadcast_to(
                np.asarray(CONST_VALS, np.float32)[None, :],
                (128, len(CONST_VALS))).copy(),
        })
    return in_maps


def kernel(**inputs):
    in_maps = stage_inputs(inputs)
    nc, _ = build(stage=4)
    res = run_bass_kernel_spmd(nc, in_maps, core_ids=list(range(8)))
    outf = np.empty((B, S, D), np.float32)
    for c in range(8):
        b, j = divmod(c, 2)
        outf[b, j * 1024:(j + 1) * 1024] = res.results[c]["out"]
    return outf


if __name__ == "__main__":
    import sys
    st = int(sys.argv[1]) if len(sys.argv) > 1 else 4
    nc, _ = build(stage=st)
    print("built stage", st, "instructions:", nc._state.n_instructions())
